# version 08
# baseline (speedup 1.0000x reference)
import contextlib
import numpy as np
import concourse.bass as bass
import concourse.mybir as mybir
from concourse.bass_utils import run_bass_kernel_spmd

F32 = mybir.dt.float32
BF16 = mybir.dt.bfloat16
AF = mybir.ActivationFunctionType
ALU = mybir.AluOpType
AX = mybir.AxisListType

L = 4096
DM = 1024
DI = 2048
DS = 16
DC = 4
DTR = 64
NE = 8
DE = 512
NCM = DM // 128
NCI = DI // 128

DEBUG = False


class Sem:
    def __init__(self, ctx, nc, name):
        self.h = ctx.enter_context(nc.semaphore(name))
        self.n = 0

    def inc(self, instr, k=1):
        instr.then_inc(self.h, k)
        self.n += k
        return self.n

    def dinc(self, instr):
        return self.inc(instr, 16)

    def wait(self, eng, val=None):
        eng.wait_ge(self.h, self.n if val is None else val)


def brow(ap_row, parts=128):
    return bass.AP(ap_row.tensor, ap_row.offset,
                   [ap_row.ap[0], [0, parts], ap_row.ap[1]])


def rep0(ap2d, reps):
    return bass.AP(ap2d.tensor, ap2d.offset,
                   [ap2d.ap[0], [0, reps], ap2d.ap[1]])


def grp0(ap2d, inner):
    return bass.AP(ap2d.tensor, ap2d.offset,
                   [ap2d.ap[0], ap2d.ap[1], [0, inner]])


def build(debug=False):
    nc = bass.Bass("TRN2", target_bir_lowering=False, debug=False)

    EI = lambda name, shape: nc.dram_tensor(name, shape, F32, kind="ExternalInput").ap()
    SCR = lambda name, shape, dt: nc.dram_tensor(
        name, shape, dt, kind=("ExternalOutput" if debug else "Internal")).ap()

    x_fm = EI("x_fm", [DM, L])
    w_inT = EI("w_inT", [DM, 2 * DI])
    conv_rel = EI("conv_rel", [128, NCI * DC])
    convb_rel = EI("convb_rel", [128, NCI])
    xprojT = EI("xprojT", [DI, 96])
    dtprojT = EI("dtprojT", [DTR, DI])
    dtb_rel = EI("dtb_rel", [128, NCI])
    A_rel = EI("A_rel", [128, NCI * DS])
    D_rel = EI("D_rel", [128, NCI])
    woutT = EI("woutT", [DI, DM])
    ln1w_rel = EI("ln1w_rel", [128, NCM])
    ln1b_rel = EI("ln1b_rel", [128, NCM])
    ffw1T = EI("ffw1T", [DM, DI])
    ffb1_rel = EI("ffb1_rel", [128, NCI])
    ffw2T = EI("ffw2T", [DI, DM])
    ffb2_rel = EI("ffb2_rel", [128, NCM])
    ln2w_rel = EI("ln2w_rel", [128, NCM])
    ln2b_rel = EI("ln2b_rel", [128, NCM])
    moe_w1 = EI("moe_w1", [NE, DM, DE])
    moeb1_rel = EI("moeb1_rel", [128, NE * 4])
    moe_w2 = EI("moe_w2", [NE, DE, DM])
    moe_b2 = EI("moe_b2", [NE, DM])
    centT = EI("centT", [DM, NE])
    wrouteT = EI("wrouteT", [DM, NE])

    out_fm = nc.dram_tensor("out_fm", [DM, L], F32, kind="ExternalOutput").ap()

    xn_bf_d = SCR("xn_bf_d", [DM, L], BF16)
    predt_d = SCR("predt_d", [DI, L], F32)
    xsc_d = SCR("xsc_d", [DI, L], BF16)
    zsil_d = SCR("zsil_d", [DI, L], BF16)
    y_d = SCR("y_d", [DI, L], BF16)
    h_d = SCR("h_d", [DM, L], BF16)
    attn_d = SCR("attn_d", [DM, L], F32)
    macc_d = nc.dram_tensor("macc_d", [DM, L], BF16, kind="Internal").ap()
    macc2_d = nc.dram_tensor("macc2_d", [DM, L], BF16, kind="Internal").ap()
    gatesv_d = nc.dram_tensor("gatesv_d", [NE, L], BF16, kind="Internal").ap()
    gates_d = SCR("gates_d", [128, 256], F32) if debug else None
    bcm_d = SCR("bcm_d", [96, L], mybir.dt.bfloat16) if debug else None
    dt_d = SCR("dt_d", [DI, L], F32) if debug else None
    hst_d = SCR("hst_d", [128, NCI * DS], F32) if debug else None
    lg_dbg = SCR("lg_dbg", [128, 256], F32) if debug else None
    gs_dbg0 = SCR("gs_dbg0", [128, 256], F32) if debug else None
    gs_dbg1 = SCR("gs_dbg1", [128, 256], F32) if debug else None
    gs_dbg2 = SCR("gs_dbg2", [128, 256], F32) if debug else None

    DB = lambda name, shape: nc.dram_tensor(name, shape, BF16, kind="Internal").ap()
    w_inT_b = DB("w_inT_b", [DM, 2 * DI])
    xprojT_b = DB("xprojT_b", [DI, 96])
    dtprojT_b = DB("dtprojT_b", [DTR, DI])
    woutT_b = DB("woutT_b", [DI, DM])
    ffw1T_b = DB("ffw1T_b", [DM, DI])
    ffw2T_b = DB("ffw2T_b", [DI, DM])
    moe_w1_b = DB("moe_w1_b", [NE, DM, DE])
    moe_w2_b = DB("moe_w2_b", [NE, DE, DM])
    moe_b2_b = DB("moe_b2_b", [NE, DM])

    ctx = contextlib.ExitStack()
    with ctx:
        sb = lambda name, shape, dt: ctx.enter_context(nc.sbuf_tensor(name, shape, dt))[:]
        ident_b = sb("ident_b", [128, 128], BF16)
        ones_b = sb("ones_b", [128, 1], BF16)
        diagD = sb("diagD", [128, NCI * 128], BF16)
        hstate = sb("hstate", [128, NCI * DS], F32)
        bcm_bf = sb("bcm_bf", [96, L], BF16)
        p_convb = sb("p_convb", [128, NCI], F32)
        p_dtb = sb("p_dtb", [128, NCI], F32)
        p_D = sb("p_D", [128, NCI], F32)
        p_ln1w = sb("p_ln1w", [128, NCM], F32)
        p_ln1b = sb("p_ln1b", [128, NCM], F32)
        p_ln2w = sb("p_ln2w", [128, NCM], F32)
        p_ln2b = sb("p_ln2b", [128, NCM], F32)
        p_ffb1 = sb("p_ffb1", [128, NCI], F32)
        p_ffb2 = sb("p_ffb2", [128, NCM], F32)
        p_moeb1 = sb("p_moeb1", [128, NE * 4], F32)
        p_invc = sb("p_invc", [NE, 1], F32)
        arel_sb = sb("arel_sb", [128, NCI * DS], F32)
        A_neg = sb("A_neg", [128, NCI * DS], F32)

        s_ld = Sem(ctx, nc, "s_ld")
        s_st = Sem(ctx, nc, "s_st")
        s_mm = Sem(ctx, nc, "s_mm")
        s_ev = Sem(ctx, nc, "s_ev")
        s_vv = Sem(ctx, nc, "s_vv")
        s_gp = Sem(ctx, nc, "s_gp")
        s_bar = Sem(ctx, nc, "s_bar")

        block = ctx.enter_context(nc.Block())
        ENGS = ("sync", "tensor", "scalar", "vector", "gpsimd")

        def barrier():
            target = s_bar.n + len(ENGS)
            for name in ENGS:
                def _sec(eng):
                    for s in (s_ld, s_st, s_mm, s_ev, s_vv, s_gp):
                        s.wait(eng)
                    eng.sem_inc(s_bar.h, 1)
                    eng.wait_ge(s_bar.h, target)
                getattr(block, name)(_sec)
            s_bar.n = target

        def st_sync(sync):
            for dst, src in ((p_convb, convb_rel), (p_dtb, dtb_rel), (p_D, D_rel),
                             (p_ln1w, ln1w_rel), (p_ln1b, ln1b_rel),
                             (p_ln2w, ln2w_rel), (p_ln2b, ln2b_rel),
                             (p_ffb1, ffb1_rel), (p_ffb2, ffb2_rel),
                             (p_moeb1, moeb1_rel),
                             (arel_sb, A_rel)):
                s_ld.dinc(sync.dma_start(out=dst, in_=src))
        block.sync(st_sync)
        ld0 = s_ld.n

        def st_gp(g):
            for ident in (ident_b,):
                g.memset(ident, 0.0)
                g.affine_select(out=ident, in_=ident, compare_op=ALU.not_equal,
                                fill=1.0, base=0, pattern=[[-1, 128]],
                                channel_multiplier=1)
            g.memset(ones_b, 1.0)
            s_gp.inc(g.memset(hstate, 0.0))
        block.gpsimd(st_gp)
        gp0 = s_gp.n

        def st_s(s):
            s.wait_ge(s_ld.h, ld0)
            s_ev.inc(s.activation(A_neg, arel_sb, AF.Exp))
        block.scalar(st_s)
        ev0 = s_ev.n

        def st_v(v):
            v.wait_ge(s_ev.h, ev0)
            v.wait_ge(s_gp.h, gp0)
            v.tensor_single_scalar(A_neg, A_neg, -1.0, ALU.mult)
            for c in range(NCI):
                ins = v.tensor_scalar(diagD[:, c * 128:(c + 1) * 128], ident_b,
                                      p_D[:, c:c + 1], None, ALU.mult)
            s_vv.inc(ins)
        block.vector(st_v)

        barrier()

        LT0 = 512
        NT0 = L // LT0
        with contextlib.ExitStack() as p0:
            psb = lambda n_, sh, dt: p0.enter_context(nc.sbuf_tensor(n_, sh, dt))[:]
            x_t = [psb(f"p0x{i}", [128, NCM * LT0], F32) for i in range(2)]
            xb_t = [psb(f"p0xb{i}", [128, NCM * LT0], BF16) for i in range(2)]
            sq_t = [psb(f"p0sq{i}", [128, NCM * LT0], BF16) for i in range(2)]
            xn_t = [psb(f"p0xn{i}", [128, NCM * LT0], BF16) for i in range(2)]
            srow = psb("p0srow", [1, 8 * LT0], F32)
            mrow = psb("p0m", [1, LT0], F32)
            rrow = psb("p0r", [1, LT0], F32)
            ixrow = psb("p0ix", [1, LT0], F32)
            m_b = psb("p0mb", [128, LT0], F32)
            r_b = psb("p0rb", [128, LT0], F32)
            ix_b = psb("p0ixb", [NE, LT0], F32)
            logits = psb("p0lg", [NE, L], F32)
            cstage = psb("p0cstg", [128, NCM * NE], F32)
            rstage = psb("p0rstg", [128, NCM * NE], F32)
            csq = psb("p0csq", [128, NCM * NE], F32)
            ones_f = psb("p0onesf", [128, 1], F32)
            cn_row = psb("p0cn", [NE, 1], F32)
            lg_tm = psb("p0lgtm", [128, 256], F32)
            ident_f = psb("p0identf", [128, 128], F32)
            gt_tm = psb("p0gttm", [128, 256], F32)
            gs = [psb(f"p0gs{i}", [128, 256], F32) for i in range(3)]
            gates_bf = psb("p0gates", [NE, L], BF16)

            x3 = lambda i: x_t[i].rearrange("p (c l) -> p c l", c=NCM)
            xb3 = lambda i: xb_t[i].rearrange("p (c l) -> p c l", c=NCM)
            sq3 = lambda i: sq_t[i].rearrange("p (c l) -> p c l", c=NCM)
            xn3 = lambda i: xn_t[i].rearrange("p (c l) -> p c l", c=NCM)

            with contextlib.ExitStack() as p0a:
                pps = lambda n_, sh: p0a.enter_context(nc.psum_tensor(n_, sh, F32))[:]
                ps_sum = pps("p0psum", [1, LT0])
                ps_sq = pps("p0psq", [1, LT0])
                ps_cos = pps("p0pcos", [NE, LT0])
                ps_rt = pps("p0prt", [NE, LT0])
                ps_cn = pps("p0pcn", [NE, 1])

                def c_sync(sync):
                    s_ld.dinc(sync.dma_start(
                        out=cstage.rearrange("p (c e) -> p c e", e=NE),
                        in_=centT.rearrange("(c p) e -> p c e", p=128)))
                    s_ld.dinc(sync.dma_start(
                        out=rstage.rearrange("p (c e) -> p c e", e=NE),
                        in_=wrouteT.rearrange("(c p) e -> p c e", p=128)))
                block.sync(c_sync)
                ldc = s_ld.n

                def c_gp(g):
                    g.memset(ones_f, 1.0)
                    g.memset(ident_f, 0.0)
                    s_gp.inc(g.affine_select(
                        out=ident_f, in_=ident_f, compare_op=ALU.not_equal,
                        fill=1.0, base=0, pattern=[[-1, 128]],
                        channel_multiplier=1))
                block.gpsimd(c_gp)
                gpc = s_gp.n

                def c_s(s):
                    s.wait_ge(s_ld.h, ldc)
                    s_ev.inc(s.activation(csq, cstage, AF.Square))
                block.scalar(c_s)
                evc = s_ev.n

                def c_t(t):
                    t.wait_ge(s_ev.h, evc)
                    t.wait_ge(s_gp.h, gpc)
                    for c in range(NCM):
                        ins = t.matmul(ps_cn, csq[:, c * NE:(c + 1) * NE], ones_f,
                                       start=(c == 0), stop=(c == NCM - 1))
                    s_mm.inc(ins)
                block.tensor(c_t)
                mmc = s_mm.n

                def c_s2(s):
                    s.wait_ge(s_mm.h, mmc)
                    s_ev.inc(s.activation(cn_row, ps_cn, AF.Sqrt))
                block.scalar(c_s2)
                evc2 = s_ev.n

                def c_v(v):
                    v.wait_ge(s_ev.h, evc2)
                    v.tensor_single_scalar(cn_row, cn_row, 1e-12, ALU.max)
                    s_vv.inc(v.reciprocal(p_invc, cn_row))
                block.vector(c_v)

                ck_xread = [0] * NT0
                ck_bfread = [0] * NT0
                ck_xn_cons = [0] * NT0
                ck_stat = [0] * NT0
                ck_logit = [0] * NT0
                ck_xnst = [0] * NT0

                for ti in range(NT0):
                    b = ti % 2
                    l0 = ti * LT0

                    def ld(sync, b=b, l0=l0, ti=ti):
                        if ti >= 2:
                            s_vv.wait(sync, ck_xread[ti - 2])
                            s_mm.wait(sync, ck_xn_cons[ti - 2])
                        s_ld.dinc(sync.dma_start(
                            out=x3(b),
                            in_=x_fm.rearrange("(c p) l -> p c l", p=128)[:, :, l0:l0 + LT0]))
                    block.sync(ld)
                    ldk = s_ld.n

                    def s1(s, b=b, ldk=ldk, ti=ti):
                        s.wait_ge(s_ld.h, ldk)
                        if ti >= 2:
                            s.wait_ge(s_mm.h, ck_bfread[ti - 2])
                        s.copy(xb_t[b], x_t[b])
                        s_ev.inc(s.activation(sq_t[b], x_t[b], AF.Square))
                    block.scalar(s1)
                    ev1 = s_ev.n

                    def mm(t, b=b, ev1=ev1, ti=ti):
                        t.wait_ge(s_ev.h, ev1)
                        for c in range(NCM):
                            t.matmul(ps_sum, ones_b, xb3(b)[:, c],
                                     start=(c == 0), stop=(c == NCM - 1))
                        for c in range(NCM):
                            ins = t.matmul(ps_sq, ones_b, sq3(b)[:, c],
                                           start=(c == 0), stop=(c == NCM - 1))
                        s_mm.inc(ins)
                    block.tensor(mm)
                    mmk = s_mm.n
                    ck_bfread[ti] = mmk

                    def s2(s, mmk=mmk):
                        s.wait_ge(s_mm.h, mmk)
                        s.mul(mrow, ps_sum, 1.0 / DM)
                        s.activation(srow[0:1, 0 * LT0:1 * LT0], mrow, AF.Square)
                        s_ev.inc(s.mul(srow[0:1, 1 * LT0:2 * LT0], ps_sq, 1.0 / DM))
                    block.scalar(s2)
                    ev2 = s_ev.n

                    def v1(v, ev2=ev2):
                        v.wait_ge(s_ev.h, ev2)
                        v.tensor_tensor(srow[0:1, 2 * LT0:3 * LT0],
                                        srow[0:1, 1 * LT0:2 * LT0],
                                        srow[0:1, 0 * LT0:1 * LT0], ALU.subtract)
                        s_vv.inc(v.tensor_single_scalar(
                            srow[0:1, 3 * LT0:4 * LT0],
                            srow[0:1, 2 * LT0:3 * LT0], 1e-5, ALU.add))
                    block.vector(v1)
                    vv1 = s_vv.n

                    def s3(s, vv1=vv1):
                        s.wait_ge(s_vv.h, vv1)
                        s_ev.inc(s.activation(srow[0:1, 4 * LT0:5 * LT0],
                                              srow[0:1, 3 * LT0:4 * LT0], AF.Sqrt))
                    block.scalar(s3)
                    ev3 = s_ev.n

                    def v2(v, ev3=ev3):
                        v.wait_ge(s_ev.h, ev3)
                        v.reciprocal(rrow, srow[0:1, 4 * LT0:5 * LT0])
                        v.tensor_tensor(srow[0:1, 5 * LT0:6 * LT0],
                                        srow[0:1, 2 * LT0:3 * LT0], rrow, ALU.mult)
                        v.tensor_tensor(srow[0:1, 5 * LT0:6 * LT0],
                                        srow[0:1, 5 * LT0:6 * LT0], rrow, ALU.mult)
                        s_vv.inc(v.tensor_single_scalar(
                            srow[0:1, 5 * LT0:6 * LT0],
                            srow[0:1, 5 * LT0:6 * LT0], float(DM), ALU.mult))
                    block.vector(v2)
                    vv2 = s_vv.n
                    ck_stat[ti] = vv2

                    def s4(s, vv2=vv2):
                        s.wait_ge(s_vv.h, vv2)
                        s_ev.inc(s.activation(srow[0:1, 6 * LT0:7 * LT0],
                                              srow[0:1, 5 * LT0:6 * LT0], AF.Sqrt))
                    block.scalar(s4)
                    ev4 = s_ev.n

                    def v2b(v, ev4=ev4):
                        v.wait_ge(s_ev.h, ev4)
                        v.tensor_single_scalar(srow[0:1, 6 * LT0:7 * LT0],
                                               srow[0:1, 6 * LT0:7 * LT0],
                                               1e-12, ALU.max)
                        s_vv.inc(v.reciprocal(ixrow, srow[0:1, 6 * LT0:7 * LT0]))
                    block.vector(v2b)
                    vv2b = s_vv.n

                    def bc(sync, vv2b=vv2b):
                        sync.wait_ge(s_vv.h, vv2b)
                        s_ld.dinc(sync.dma_start(out=m_b, in_=brow(mrow)))
                        s_ld.dinc(sync.dma_start(out=r_b, in_=brow(rrow)))
                        s_ld.dinc(sync.dma_start(out=ix_b, in_=brow(ixrow, NE)))
                    block.sync(bc)
                    bck = s_ld.n

                    def v3(v, b=b, bck=bck, ti=ti):
                        v.wait_ge(s_ld.h, bck)
                        if ti >= 2:
                            v.wait_ge(s_st.h, ck_xnst[ti - 2])
                            v.wait_ge(s_mm.h, ck_xn_cons[ti - 2])
                        v.tensor_tensor(x3(b), x3(b), rep0(m_b, NCM), ALU.subtract)
                        v.tensor_tensor(x3(b), x3(b), rep0(r_b, NCM), ALU.mult)
                        s_vv.inc(v.tensor_copy(xn_t[b], x_t[b]))
                    block.vector(v3)
                    xnk = s_vv.n
                    ck_xread[ti] = xnk

                    def mm2(t, b=b, xnk=xnk, ti=ti):
                        t.wait_ge(s_vv.h, xnk)
                        if ti >= 1:
                            t.wait_ge(s_vv.h, ck_logit[ti - 1])
                        for c in range(NCM):
                            t.matmul(ps_cos, cstage[:, c * NE:(c + 1) * NE], x3(b)[:, c],
                                     start=(c == 0), stop=(c == NCM - 1))
                        for c in range(NCM):
                            ins = t.matmul(ps_rt, rstage[:, c * NE:(c + 1) * NE],
                                           x3(b)[:, c], start=(c == 0), stop=(c == NCM - 1))
                        s_mm.inc(ins)
                    block.tensor(mm2)
                    mm2k = s_mm.n
                    ck_xn_cons[ti] = mm2k

                    def v4(v, mm2k=mm2k, l0=l0):
                        v.wait_ge(s_mm.h, mm2k)
                        lg = logits[:, l0:l0 + LT0]
                        v.tensor_scalar(ps_cos, ps_cos, p_invc, None, ALU.mult)
                        v.tensor_tensor(lg, ps_cos, ix_b, ALU.mult)
                        s_vv.inc(v.tensor_tensor(lg, lg, ps_rt, ALU.add))
                    block.vector(v4)
                    ck_logit[ti] = s_vv.n

                    def stx(sync, b=b, l0=l0, ti=ti):
                        s_vv.wait(sync, ck_xread[ti])
                        s_st.dinc(sync.dma_start(
                            out=xn_bf_d.rearrange("(c p) l -> p c l", p=128)[:, :, l0:l0 + LT0],
                            in_=xn3(b)))
                    block.sync(stx)
                    ck_xnst[ti] = s_st.n

            with contextlib.ExitStack() as p0b:
                ps_tp = [p0b.enter_context(nc.psum_tensor(f"p0tp{i}", [128, 512], F32))[:][:, 0:8]
                         for i in range(2)]
                ps_tq = [p0b.enter_context(nc.psum_tensor(f"p0tq{i}", [128, 512], F32))[:][0:NE, 0:128]
                         for i in range(2)]

                MBg = s_mm.n
                EvBg = s_ev.n

                def g_t(t, MBg=MBg, EvBg=EvBg):
                    s_vv.wait(t)
                    s_gp.wait(t)
                    for blk in range(32):
                        if blk >= 2:
                            t.wait_ge(s_ev.h, EvBg + blk - 1)
                        s_mm.inc(t.transpose(ps_tp[blk % 2],
                                             logits[:, blk * 128:(blk + 1) * 128],
                                             ident_f[0:8, 0:8]))
                block.tensor(g_t)
                assert s_mm.n == MBg + 32, ('s_mm', s_mm.n, MBg + 32)

                def g_s(s, MBg=MBg):
                    for blk in range(32):
                        s.wait_ge(s_mm.h, MBg + blk + 1)
                        s_ev.inc(s.copy(lg_tm[:, blk * 8:(blk + 1) * 8], ps_tp[blk % 2]))
                block.scalar(g_s)
                assert s_ev.n == EvBg + 32, ('s_ev', s_ev.n, EvBg + 32)
                evg = s_ev.n

                def g_v(v):
                    v.wait_ge(s_ev.h, evg)
                    l3 = lg_tm.rearrange("p (g e) -> p g e", e=8)
                    m1 = gs[0][:, 0:32]
                    v.reduce_max(m1.rearrange("p (g o) -> p g o", o=1), l3, axis=AX.X)
                    d = gs[1]
                    v.tensor_tensor(d.rearrange("p (g e) -> p g e", e=8), l3,
                                    grp0(m1, 8), ALU.subtract)
                    z = gs[2]
                    v.tensor_single_scalar(z, d, 0.0, ALU.is_equal)
                    v.scalar_tensor_tensor(z, z, -1e30, d, ALU.mult, ALU.add)
                    m2 = gs[0][:, 32:64]
                    v.reduce_max(m2.rearrange("p (g o) -> p g o", o=1),
                                 z.rearrange("p (g e) -> p g e", e=8), axis=AX.X)
                    keep = gs[2]
                    s_vv.inc(v.tensor_tensor(keep.rearrange("p (g e) -> p g e", e=8),
                                             d.rearrange("p (g e) -> p g e", e=8),
                                             grp0(m2, 8), ALU.is_ge))
                block.vector(g_v)
                vvg = s_vv.n

                def g_v1b(v):
                    v.wait_ge(s_vv.h, vvg)
                    m2 = gs[0][:, 32:64]
                    s_vv.inc(v.scalar_tensor_tensor(
                        gs[1].rearrange("p (g e) -> p g e", e=8),
                        gs[1].rearrange("p (g e) -> p g e", e=8),
                        2.0, grp0(m2, 8), ALU.mult, ALU.subtract))
                block.vector(g_v1b)
                vvg1b = s_vv.n

                def g_s2(s):
                    s.wait_ge(s_vv.h, vvg1b)
                    s_ev.inc(s.activation(gs[1], gs[1], AF.Sigmoid))
                block.scalar(g_s2)
                evg2 = s_ev.n

                def g_v2(v):
                    v.wait_ge(s_ev.h, evg2)
                    s_vv.inc(v.tensor_tensor(gt_tm, gs[1], gs[2], ALU.mult))
                block.vector(g_v2)
                vvg2 = s_vv.n

                MBg2 = s_mm.n
                EvBg2 = s_ev.n

                def g_t2(t, MBg2=MBg2, EvBg2=EvBg2):
                    t.wait_ge(s_vv.h, vvg2)
                    for blk in range(32):
                        if blk >= 2:
                            t.wait_ge(s_ev.h, EvBg2 + blk - 1)
                        s_mm.inc(t.transpose(ps_tq[blk % 2],
                                             gt_tm[:, blk * 8:(blk + 1) * 8], ident_f))
                block.tensor(g_t2)
                assert s_mm.n == MBg2 + 32, ('s_mm', s_mm.n, MBg2 + 32)

                def g_s3(s, MBg2=MBg2):
                    for blk in range(32):
                        s.wait_ge(s_mm.h, MBg2 + blk + 1)
                        s_ev.inc(s.copy(gates_bf[:, blk * 128:(blk + 1) * 128],
                                        ps_tq[blk % 2]))
                block.scalar(g_s3)
                assert s_ev.n == EvBg2 + 32, ('s_ev', s_ev.n, EvBg2 + 32)

                def g_spill(sync, EvBg2=EvBg2):
                    sync.wait_ge(s_ev.h, EvBg2 + 32)
                    s_st.dinc(sync.dma_start(out=gatesv_d, in_=gates_bf))
                block.sync(g_spill)

                if debug:
                    def g_dbg(sync):
                        s_ev.wait(sync)
                        s_vv.wait(sync)
                        s_st.dinc(sync.dma_start(out=gates_d, in_=gt_tm))
                        s_st.dinc(sync.dma_start(out=lg_dbg, in_=lg_tm))
                        s_st.dinc(sync.dma_start(out=gs_dbg0, in_=gs[0]))
                        s_st.dinc(sync.dma_start(out=gs_dbg1, in_=gs[1]))
                        s_st.dinc(sync.dma_start(out=gs_dbg2, in_=gs[2]))
                    block.sync(g_dbg)

            barrier()

        with contextlib.ExitStack() as pw:
            wsb = lambda n_, sh, dt: pw.enter_context(nc.sbuf_tensor(n_, sh, dt))[:]
            NS = 3
            stg = [wsb(f"wstg{i}", [128, 8192], F32) for i in range(NS)]
            stb = [wsb(f"wstb{i}", [128, 8192], BF16) for i in range(NS)]

            jobs = []
            def addjob(src, dst, rows, cols, maxchunks=8):
                nch = rows // 128 if rows >= 128 else 1
                pr = 128 if rows >= 128 else rows
                c0 = 0
                while c0 < max(nch, 1):
                    take = min(maxchunks, nch - c0)
                    if rows >= 128:
                        jobs.append((
                            src.rearrange("(c p) e -> p c e", p=128)[:, c0:c0 + take],
                            dst.rearrange("(c p) e -> p c e", p=128)[:, c0:c0 + take],
                            128, take, cols))
                    else:
                        jobs.append((src, dst, rows, 1, cols))
                    c0 += take

            addjob(w_inT, w_inT_b, DM, 2 * DI, maxchunks=2)
            addjob(xprojT, xprojT_b, DI, 96)
            addjob(dtprojT, dtprojT_b, DTR, DI)
            addjob(woutT, woutT_b, DI, DM)
            addjob(ffw1T, ffw1T_b, DM, DI, maxchunks=4)
            addjob(ffw2T, ffw2T_b, DI, DM)
            for e in range(NE):
                addjob(moe_w1[e], moe_w1_b[e], DM, DE)
                addjob(moe_w2[e], moe_w2_b[e], DE, DM)
            addjob(moe_b2, moe_b2_b, NE, DM)

            ck_wev = [0] * len(jobs)
            ck_wst = [0] * len(jobs)
            for ji, (srcap, dstap, nr, nch, ncl) in enumerate(jobs):
                bw = ji % NS
                tot = nch * ncl

                def w_ld(sync, srcap=srcap, bw=bw, ji=ji, nr=nr, nch=nch, ncl=ncl):
                    if ji >= NS:
                        s_ev.wait(sync, ck_wev[ji - NS])
                    dst = stg[bw][0:nr, 0:nch * ncl]
                    if nch > 1:
                        dst = dst.rearrange("p (c e) -> p c e", c=nch)
                    s_ld.dinc(sync.dma_start(out=dst, in_=srcap))
                block.sync(w_ld)
                ldkw = s_ld.n

                def w_cast(s, bw=bw, ldkw=ldkw, ji=ji, nr=nr, tot=tot):
                    s.wait_ge(s_ld.h, ldkw)
                    if ji >= NS:
                        s_st.wait(s, ck_wst[ji - NS])
                    s_ev.inc(s.copy(stb[bw][0:nr, 0:tot], stg[bw][0:nr, 0:tot]))
                block.scalar(w_cast)
                ck_wev[ji] = s_ev.n
                evkw = s_ev.n

                def w_st(sync, bw=bw, evkw=evkw, dstap=dstap, nr=nr, nch=nch, ncl=ncl):
                    sync.wait_ge(s_ev.h, evkw)
                    src = stb[bw][0:nr, 0:nch * ncl]
                    if nch > 1:
                        src = src.rearrange("p (c e) -> p c e", c=nch)
                    s_st.dinc(sync.dma_start(out=dstap, in_=src))
                block.sync(w_st)
                ck_wst[ji] = s_st.n

            barrier()

        LT1 = 512
        NT1 = L // LT1
        with contextlib.ExitStack() as p1:
            psb = lambda n_, sh, dt: p1.enter_context(nc.sbuf_tensor(n_, sh, dt))[:]
            w_in_sb = psb("p1win", [128, NCM * 2 * DI], BF16)
            xproj_sb = psb("p1xp", [128, NCI * 96], BF16)
            dtproj_sb = psb("p1dtp", [DTR, DI], BF16)
            diagK = psb("p1diagK", [128, NCI * DC * 128], BF16)
            conv_sb = psb("p1convsb", [128, NCI * DC], F32)
            xn_t = [psb(f"p1xn{i}", [128, NCM * LT1], BF16) for i in range(2)]
            xsraw = [psb(f"p1xsr{i}", [128, NCI * (LT1 + 3)], BF16) for i in range(2)]
            zs_r = [psb(f"p1zs{i}", [128, LT1], BF16) for i in range(4)]
            xc_r = [psb(f"p1xc{i}", [128, LT1], BF16) for i in range(4)]
            pd_r = [psb(f"p1pd{i}", [128, LT1], F32) for i in range(4)]
            dtr_sb = [psb(f"p1dtr{i}", [DTR, LT1], BF16) for i in range(2)]
            pb = [p1.enter_context(nc.psum_tensor(f"p1ps{i}", [128, LT1], F32))[:]
                  for i in range(8)]
            xsr3 = lambda i: xsraw[i].rearrange("p (c l) -> p c l", c=NCI)

            def cv_ld(sync):
                s_ld.dinc(sync.dma_start(out=conv_sb, in_=conv_rel))
            block.sync(cv_ld)
            cvldk = s_ld.n

            def d_v(v):
                v.wait_ge(s_ld.h, cvldk)
                for c in range(NCI):
                    for k in range(DC):
                        ins = v.tensor_scalar(
                            diagK[:, (c * DC + k) * 128:(c * DC + k + 1) * 128],
                            ident_b, conv_sb[:, c * DC + k:c * DC + k + 1],
                            None, ALU.mult)
                for i in range(2):
                    for c in range(NCI):
                        ins = v.memset(xsr3(i)[:, c, 0:3], 0.0)
                s_vv.inc(ins)
            block.vector(d_v)
            vvd = s_vv.n

            def w_ld(sync):
                s_ld.dinc(sync.dma_start(
                    out=w_in_sb.rearrange("p (c e) -> p c e", c=NCM), in_=w_inT_b.rearrange("(c p) e -> p c e", p=128)))
                s_ld.dinc(sync.dma_start(
                    out=xproj_sb.rearrange("p (c e) -> p c e", c=NCI), in_=xprojT_b.rearrange("(c p) e -> p c e", p=128)))
                s_ld.dinc(sync.dma_start(out=dtproj_sb, in_=dtprojT_b))
            block.sync(w_ld)
            wldk = s_ld.n

            bank_ck = [(None, 0)] * 8
            ck_xn_cons = [0] * NT1
            ck_xsraw_cons = [0] * NT1
            ck_xc_prev = {}
            ck_halo = [vvd] * NT1

            def bank_wait(t, bank):
                sem, cnt = bank_ck[bank]
                if sem is not None:
                    t.wait_ge(sem.h, cnt)

            for ti in range(NT1):
                b = ti % 2
                l0 = ti * LT1

                def ld(sync, b=b, l0=l0, ti=ti):
                    if ti >= 2:
                        s_mm.wait(sync, ck_xn_cons[ti - 2])
                    s_ld.dinc(sync.dma_start(
                        out=xn_t[b].rearrange("p (c l) -> p c l", c=NCM),
                        in_=xn_bf_d.rearrange("(c p) l -> p c l", p=128)[:, :, l0:l0 + LT1]))
                block.sync(ld)
                ldk = s_ld.n

                if ti > 0:
                    def halo(v, b=b, ti=ti):
                        s_ev.wait(v)
                        if ti >= 2:
                            v.wait_ge(s_mm.h, ck_xsraw_cons[ti - 2])
                        for c in range(NCI):
                            ins = v.tensor_copy(xsr3(b)[:, c, 0:3],
                                                xsr3(1 - b)[:, c, LT1:LT1 + 3])
                        s_vv.inc(ins)
                    block.vector(halo)
                    ck_halo[ti] = s_vv.n

                MBa = s_mm.n
                EvBa = s_ev.n

                def t_xs(t, b=b, ldk=ldk, ti=ti, EvBa=EvBa):
                    t.wait_ge(s_ld.h, max(ldk, wldk))
                    t.wait_ge(s_vv.h, ck_halo[ti])
                    for m in range(16):
                        bank = m % 7
                        if m >= 7:
                            t.wait_ge(s_ev.h, EvBa + m - 7 + 1)
                        else:
                            sem, cnt = bank_ck[bank]
                            if sem is not None:
                                t.wait_ge(sem.h, cnt)
                        for k in range(NCM):
                            ins = t.matmul(
                                pb[bank],
                                w_in_sb[:, k * 2 * DI + m * 128:k * 2 * DI + (m + 1) * 128],
                                xn_t[b].rearrange("p (c l) -> p c l", c=NCM)[:, k],
                                start=(k == 0), stop=(k == NCM - 1))
                        s_mm.inc(ins)
                block.tensor(t_xs)
                assert s_mm.n == MBa + 16, ('s_mm', s_mm.n, MBa + 16)

                def s_xs(s, b=b, MBa=MBa):
                    for m in range(16):
                        s.wait_ge(s_mm.h, MBa + m + 1)
                        s_ev.inc(s.copy(xsr3(b)[:, m, 3:3 + LT1], pb[m % 7]))
                block.scalar(s_xs)
                assert s_ev.n == EvBa + 16, ('s_ev', s_ev.n, EvBa + 16)
                for m in range(9, 16):
                    bank_ck[m % 7] = (s_ev, EvBa + m + 1)

                MBb = s_mm.n
                EvBb = s_ev.n
                SBb = s_st.n

                def t_cv(t, b=b, EvBb=EvBb, MBb=MBb):
                    sem, cnt = bank_ck[7]
                    if sem is not None:
                        t.wait_ge(sem.h, cnt)
                    for c in range(NCI):
                        bank = c % 7
                        if c >= 7:
                            t.wait_ge(s_ev.h, EvBb + c - 7 + 1)
                        else:
                            sem, cnt = bank_ck[bank]
                            if sem is not None:
                                t.wait_ge(sem.h, cnt)
                        for k in range(DC):
                            ins = t.matmul(
                                pb[bank], diagK[:, (c * DC + k) * 128:(c * DC + k + 1) * 128],
                                xsr3(b)[:, c, k:k + LT1],
                                start=(k == 0), stop=(k == DC - 1))
                        s_mm.inc(ins)
                        t.wait_ge(s_ev.h, EvBb + c + 1)
                        s_mm.inc(t.matmul(pb[7][0:96, :], xproj_sb[:, c * 96:(c + 1) * 96],
                                          xc_r[c % 4], start=(c == 0), stop=(c == NCI - 1)))
                block.tensor(t_cv)
                assert s_mm.n == MBb + 32, ('s_mm', s_mm.n, MBb + 32)

                def s_cv(s, b=b, l0=l0, MBb=MBb, EvBb=EvBb, SBb=SBb):
                    for c in range(NCI):
                        s.wait_ge(s_mm.h, MBb + 2 * c + 1)
                        if c >= 4:
                            s.wait_ge(s_mm.h, MBb + 2 * (c - 4) + 2)
                            s.wait_ge(s_st.h, SBb + 16 * (c - 3))
                        else:
                            key = ("xc", c)
                            if key in ck_xc_prev:
                                s.wait_ge(s_mm.h, ck_xc_prev[key][0])
                                s.wait_ge(s_st.h, ck_xc_prev[key][1])
                        s_ev.inc(s.activation(xc_r[c % 4], pb[c % 7], AF.Silu,
                                              bias=p_convb[:, c:c + 1]))
                    s.wait_ge(s_mm.h, MBb + 32)
                    s.copy(dtr_sb[b], pb[7][0:DTR, :])
                    s_ev.inc(s.copy(bcm_bf[64:96, l0:l0 + LT1], pb[7][64:96, :]))
                block.scalar(s_cv)
                assert s_ev.n == EvBb + 17, ('s_ev', s_ev.n, EvBb + 17)
                for c in range(9, 16):
                    bank_ck[c % 7] = (s_ev, EvBb + c + 1)
                bank_ck[7] = (s_ev, EvBb + 17)
                xp_ev = EvBb + 17
                for c in range(12, 16):
                    ck_xc_prev[("xc", c - 12)] = (MBb + 2 * c + 2, SBb + 16 * (c + 1))

                def st_xc(sync, l0=l0, EvBb=EvBb):
                    for c in range(NCI):
                        sync.wait_ge(s_ev.h, EvBb + c + 1)
                        s_st.dinc(sync.dma_start(
                            out=xsc_d[c * 128:(c + 1) * 128, l0:l0 + LT1], in_=xc_r[c % 4]))
                block.sync(st_xc)
                assert s_st.n == SBb + 16 * 16, ('s_st', s_st.n, SBb + 16 * 16)

                MBc = s_mm.n
                EvBc = s_ev.n
                SBc = s_st.n

                def t_dt(t, b=b, xp_ev=xp_ev, EvBc=EvBc):
                    t.wait_ge(s_ev.h, xp_ev)
                    for m in range(NCI):
                        bank = m % 7
                        if m >= 7:
                            t.wait_ge(s_ev.h, EvBc + m - 7 + 1)
                        else:
                            sem, cnt = bank_ck[bank]
                            if sem is not None:
                                t.wait_ge(sem.h, cnt)
                        s_mm.inc(t.matmul(pb[bank], dtproj_sb[:, m * 128:(m + 1) * 128],
                                          dtr_sb[b], start=True, stop=True))
                block.tensor(t_dt)
                assert s_mm.n == MBc + 16, ('s_mm', s_mm.n, MBc + 16)

                def s_dt(s, MBc=MBc, SBc=SBc, b=b):
                    for m in range(NCI):
                        s.wait_ge(s_mm.h, MBc + m + 1)
                        if m >= 4:
                            s.wait_ge(s_st.h, SBc + 16 * (m - 3))
                        else:
                            key = ("pd", m)
                            if key in ck_xc_prev:
                                s.wait_ge(s_st.h, ck_xc_prev[key])
                        s_ev.inc(s.activation(pd_r[m % 4], pb[m % 7], AF.Identity,
                                              bias=p_dtb[:, m:m + 1]))
                block.scalar(s_dt)
                assert s_ev.n == EvBc + 16, ('s_ev', s_ev.n, EvBc + 16)
                for m in range(9, 16):
                    bank_ck[m % 7] = (s_ev, EvBc + m + 1)
                for m in range(12, 16):
                    ck_xc_prev[("pd", m - 12)] = SBc + 16 * (m + 1)

                def st_pd(sync, l0=l0, EvBc=EvBc):
                    for m in range(NCI):
                        sync.wait_ge(s_ev.h, EvBc + m + 1)
                        s_st.dinc(sync.dma_start(
                            out=predt_d[m * 128:(m + 1) * 128, l0:l0 + LT1], in_=pd_r[m % 4]))
                block.sync(st_pd)
                assert s_st.n == SBc + 16 * 16, ('s_st', s_st.n, SBc + 16 * 16)

                MBd = s_mm.n
                EvBd = s_ev.n
                SBd = s_st.n

                def t_z(t, b=b, EvBd=EvBd):
                    for m in range(16):
                        bank = m % 7
                        if m >= 7:
                            t.wait_ge(s_ev.h, EvBd + m - 7 + 1)
                        else:
                            sem, cnt = bank_ck[bank]
                            if sem is not None:
                                t.wait_ge(sem.h, cnt)
                        for k in range(NCM):
                            ins = t.matmul(
                                pb[bank],
                                w_in_sb[:, k * 2 * DI + (16 + m) * 128:k * 2 * DI + (17 + m) * 128],
                                xn_t[b].rearrange("p (c l) -> p c l", c=NCM)[:, k],
                                start=(k == 0), stop=(k == NCM - 1))
                        s_mm.inc(ins)
                block.tensor(t_z)
                assert s_mm.n == MBd + 16, ('s_mm', s_mm.n, MBd + 16)
                ck_xn_cons[ti] = MBd + 16
                ck_xsraw_cons[ti] = MBb + 32

                def s_z(s, MBd=MBd, SBd=SBd, b=b):
                    for m in range(16):
                        s.wait_ge(s_mm.h, MBd + m + 1)
                        if m >= 4:
                            s.wait_ge(s_st.h, SBd + 16 * (m - 3))
                        else:
                            key = ("zs", m)
                            if key in ck_xc_prev:
                                s.wait_ge(s_st.h, ck_xc_prev[key])
                        s_ev.inc(s.activation(zs_r[m % 4], pb[m % 7], AF.Silu))
                block.scalar(s_z)
                assert s_ev.n == EvBd + 16, ('s_ev', s_ev.n, EvBd + 16)
                for m in range(9, 16):
                    bank_ck[m % 7] = (s_ev, EvBd + m + 1)
                for m in range(12, 16):
                    ck_xc_prev[("zs", m - 12)] = SBd + 16 * (m + 1)

                def st_z(sync, l0=l0, EvBd=EvBd):
                    for m in range(16):
                        sync.wait_ge(s_ev.h, EvBd + m + 1)
                        s_st.dinc(sync.dma_start(
                            out=zsil_d[m * 128:(m + 1) * 128, l0:l0 + LT1], in_=zs_r[m % 4]))
                block.sync(st_z)
                assert s_st.n == SBd + 16 * 16, ('s_st', s_st.n, SBd + 16 * 16)

            barrier()

        LT2 = 1024
        NT2 = L // LT2
        with contextlib.ExitStack() as p2:
            psb = lambda n_, sh, dt: p2.enter_context(nc.sbuf_tensor(n_, sh, dt))[:]
            pd_one = psb("p2pd", [128, LT2], F32)
            pd_t = [pd_one, pd_one]
            xs_t = [psb(f"p2xs{i}", [128, LT2], BF16) for i in range(2)]
            tA = psb("p2tA", [128, LT2], F32)
            dt_one = psb("p2dt", [128, LT2], F32)
            dt_f = [dt_one, dt_one]
            dtxs_one = psb("p2dtxs", [128, LT2], BF16)
            dtxs = [dtxs_one, dtxs_one]
            bb_one = psb("p2bb", [128, DS * LT2], BF16)
            cb_one = psb("p2cb", [128, DS * LT2], BF16)
            bb_t = [bb_one, bb_one]
            cb_t = [cb_one, cb_one]
            da_t = [psb(f"p2da{i}", [128, LT2], F32) for i in range(2)]
            dbx_t = [psb(f"p2dbx{i}", [128, LT2], BF16) for i in range(2)]
            h_t = [psb(f"p2h{i}", [128, LT2], BF16) for i in range(2)]
            hc_t = [psb(f"p2hc{i}", [128, LT2], BF16) for i in range(2)]
            y_one = psb("p2y", [128, LT2], BF16)
            y_t = [y_one, y_one]
            py = [p2.enter_context(nc.psum_tensor(f"p2py{i}", [128, LT2], F32))[:]
                  for i in range(2)]
            LT5 = 256
            NT5 = L // LT5
            m_w1 = psb("p5w1", [128, 4 * NCM * DE], BF16)
            m_w2 = psb("p5w2", [128, 4 * 4 * DM], BF16)
            m_b2 = psb("p5b2", [NE, DM], BF16)
            m_xn = psb("p5xn", [128, NCM * LT5], BF16)
            m_gam = psb("p5gam", [128, 16 * LT5], BF16)
            m_gb = psb("p5gb", [128, 4 * LT5], BF16)
            m_acc = psb("p5acc", [128, NCM * LT5], BF16)
            g_sl = [psb(f"p5gsl{i}", [NE, LT5], BF16) for i in range(2)]
            p5p1 = [p2.enter_context(
                nc.psum_tensor(f"p5ps1{i}", [128, 512], F32))[:][:, 0:LT5]
                for i in range(2)]
            p5p2 = [p2.enter_context(
                nc.psum_tensor(f"p5ps2{i}", [128, 512], F32))[:][:, 0:LT5]
                for i in range(2)]
            m_acc3 = m_acc.rearrange("p (q l) -> p q l", q=NCM)

            p5ck = {}
            wld1 = [0, 0]
            wld2 = [0, 0]

            def p5_ld(j):
                half5, ti5 = j // 16, j % 16
                es5 = (0, 1, 2, 3) if half5 == 0 else (4, 5, 6, 7)
                b5, l05 = j % 2, ti5 * LT5
                if j == 0:
                    def wl0(sync):
                        s_ld.dinc(sync.dma_start(out=m_b2, in_=moe_b2_b))
                        for i, e in enumerate(es5):
                            s_ld.dinc(sync.dma_start(
                                out=m_w1[:, i * NCM * DE:(i + 1) * NCM * DE].rearrange(
                                    "p (q j) -> p q j", q=NCM),
                                in_=moe_w1_b[e].rearrange("(q p) j -> p q j", p=128)))
                        wld1[0] = s_ld.n
                        for i, e in enumerate(es5):
                            s_ld.dinc(sync.dma_start(
                                out=m_w2[:, i * 4 * DM:(i + 1) * 4 * DM].rearrange(
                                    "p (q d) -> p q d", q=4),
                                in_=moe_w2_b[e].rearrange("(q p) d -> p q d", p=128)))
                        wld2[0] = s_ld.n
                    block.sync(wl0)
                if j == 16:
                    def wl1(sync):
                        s_mm.wait(sync, p5ck[15]["t15_end"])
                        for i, e in enumerate(es5):
                            s_ld.dinc(sync.dma_start(
                                out=m_w1[:, i * NCM * DE:(i + 1) * NCM * DE].rearrange(
                                    "p (q j) -> p q j", q=NCM),
                                in_=moe_w1_b[e].rearrange("(q p) j -> p q j", p=128)))
                        wld1[1] = s_ld.n
                    block.sync(wl1)

                def ld5(sync, j=j, b5=b5, l05=l05, es5=es5):
                    if j >= 1:
                        s_mm.wait(sync, p5ck[j - 1]["t15_end"])
                        s_gp.wait(sync, p5ck[j - 1]["gate"])
                    s_ld.dinc(sync.dma_start(out=g_sl[b5],
                                             in_=gatesv_d[:, l05:l05 + LT5]))
                    gslk = s_ld.n
                    s_ld.dinc(sync.dma_start(
                        out=m_xn.rearrange("p (q l) -> p q l", q=NCM),
                        in_=xn_bf_d.rearrange("(q p) l -> p q l",
                                              p=128)[:, :, l05:l05 + LT5]))
                    sync.wait_ge(s_ld.h, gslk)
                    for i, e in enumerate(es5):
                        s_ld.dinc(sync.dma_start(
                            out=m_gb[:, i * LT5:(i + 1) * LT5],
                            in_=brow(g_sl[b5][e:e + 1, :])))
                block.sync(ld5)
                p5ck[j] = {"ld": s_ld.n}

            ck_vend = [0] * NT2
            ck_gend = [0] * NT2
            prev_gp = [None]
            prev_dtv = [0]
            prev_hc = [None]
            ck_pd = {}
            ck_xs = {}
            ck_hc = []
            ck_da = {}
            ck_y = {}
            ck_yst = {}

            if debug:
                def bcmdump(sync):
                    s_st.dinc(sync.dma_start(out=bcm_d, in_=bcm_bf))
                block.sync(bcmdump)

            hstdump_ck = [0]
            for lt in range(NT2):
                l0 = lt * LT2
                lb = lt % 2

                def bc(sync, l0=l0, lt=lt, lb=lb):
                    if lt >= 1:
                        s_vv.wait(sync, ck_vend[lt - 1])
                        s_gp.wait(sync, ck_gend[lt - 1])
                    for n in range(DS):
                        s_ld.dinc(sync.dma_start(
                            out=bb_t[lb][:, n * LT2:(n + 1) * LT2],
                            in_=brow(bcm_bf[64 + n:65 + n, l0:l0 + LT2])))
                        s_ld.dinc(sync.dma_start(
                            out=cb_t[lb][:, n * LT2:(n + 1) * LT2],
                            in_=brow(bcm_bf[80 + n:81 + n, l0:l0 + LT2])))
                block.sync(bc)
                bck = s_ld.n

                for c in range(NCI):
                    cb = c % 2
                    seq = lt * NCI + c

                    def ld(sync, c=c, cb=cb, l0=l0, lt=lt, seq=seq):
                        if seq >= 1:
                            pc, pl = (c - 1, lt) if c >= 1 else (NCI - 1, lt - 1)
                            s_ev.wait(sync, ck_pd[(pl, pc)])
                        if seq >= 2:
                            pc, pl = (c - 2, lt) if c >= 2 else (NCI - 2 + c, lt - 1)
                            s_mm.wait(sync, ck_xs[(pl, pc)][0])
                            s_vv.wait(sync, ck_xs[(pl, pc)][1])
                        s_ld.dinc(sync.dma_start(
                            out=pd_t[cb], in_=predt_d[c * 128:(c + 1) * 128, l0:l0 + LT2]))
                        s_ld.dinc(sync.dma_start(
                            out=xs_t[cb], in_=xsc_d[c * 128:(c + 1) * 128, l0:l0 + LT2]))
                    block.sync(ld)
                    ldk = s_ld.n

                    j15 = seq // 2 - 1 if (seq % 2 == 0 and seq >= 2) else None
                    if j15 is not None and j15 > 30:
                        j15 = None
                    j25 = (seq - 3) // 2 if (seq % 2 == 1 and seq >= 3) else None
                    if j25 is not None and j25 > 30:
                        j25 = None
                    jld = (seq - 1) // 2 if seq % 2 == 1 else None
                    if jld is not None and jld <= 30:
                        p5_ld(jld)
                    if seq == 34:
                        def wl2(sync):
                            s_mm.wait(sync, p5ck[15]["t25_end"])
                            for i, e in enumerate((4, 5, 6, 7)):
                                s_ld.dinc(sync.dma_start(
                                    out=m_w2[:, i * 4 * DM:(i + 1) * 4 * DM].rearrange(
                                        "p (q d) -> p q d", q=4),
                                    in_=moe_w2_b[e].rearrange("(q p) d -> p q d",
                                                              p=128)))
                            wld2[1] = s_ld.n
                        block.sync(wl2)

                    EvB = s_ev.n
                    VB = s_vv.n
                    MB = s_mm.n
                    if j15 is not None:
                        def daev(n, EvB=EvB):
                            return EvB + 2 + n if n < 2 else EvB + 2 * n
                        def gelinc(g, EvB=EvB):
                            return EvB + 2 * g + 5 if g <= 13 else EvB + 19 + g
                        accinc = None
                        evtot = 34
                        hc_abs = [MB + 2 * n + 2 for n in range(DS)]
                    elif j25 is not None:
                        def daev(n, EvB=EvB):
                            return EvB + 2 + n + max(0, n - 2) // 2
                        def accinc(m, EvB=EvB):
                            return EvB + 3 * m + 6 if m <= 6 else EvB + 26
                        gelinc = None
                        evtot = 26
                        hc_abs = [MB + 3 * (n // 2) + 1 if n % 2 == 0
                                  else MB + 3 * ((n - 1) // 2) + 3
                                  for n in range(DS)]
                    else:
                        def daev(n, EvB=EvB):
                            return EvB + 2 + n
                        gelinc = accinc = None
                        evtot = 18
                        hc_abs = [MB + n + 1 for n in range(DS)]

                    def p2s(sc, cb=cb, ldk=ldk, seq=seq, c=c, lt=lt, EvB=EvB,
                            VB=VB, MB=MB, j15=j15, j25=j25):
                        if j15 is not None:
                            es5 = (0, 1, 2, 3) if j15 < 16 else (4, 5, 6, 7)

                            def gel(g):
                                sc.wait_ge(s_mm.h, MB + 2 * g + 1)
                                if g < 2 and j15 >= 1:
                                    sc.wait_ge(s_mm.h, p5ck[j15 - 1]["t25_end"])
                                e5 = es5[g // 4]
                                s_ev.inc(sc.activation(
                                    m_gam[:, g * LT5:(g + 1) * LT5], p5p1[g % 2],
                                    AF.Gelu,
                                    bias=p_moeb1[:, e5 * 4 + g % 4:
                                                 e5 * 4 + g % 4 + 1]))
                        if j25 is not None:
                            def acc(m):
                                sc.wait_ge(s_mm.h, MB + 3 * m + 2)
                                if m == 0 and j25 >= 1:
                                    sc.wait_ge(s_st.h, p5ck[j25 - 1]["st"])
                                s_ev.inc(sc.copy(m_acc3[:, m], p5p2[m % 2]))
                        sc.wait_ge(s_ld.h, ldk)
                        if seq >= 1:
                            sc.wait_ge(s_vv.h, prev_dtv[0])
                        sc.activation(tA, pd_t[cb], AF.Abs)
                        sc.activation(tA, tA, AF.Exp, scale=-1.0)
                        sc.activation(tA, tA, AF.Ln, bias=1.0)
                        s_ev.inc(sc.activation(dt_f[cb], pd_t[cb], AF.Relu))
                        for n in range(DS):
                            nb = n % 2
                            col = c * DS + n
                            if n == 0:
                                sc.wait_ge(s_vv.h, VB + 1)
                            if n >= 2:
                                sc.wait_ge(s_vv.h, VB + n)
                                sc.copy(hstate[:, col - 2:col - 1],
                                        h_t[nb][:, LT2 - 1:LT2])
                            s_ev.inc(sc.activation(da_t[nb], dt_f[cb], AF.Exp,
                                                   scale=A_neg[:, col:col + 1]))
                            if j15 is not None and n >= 2:
                                gel(n - 2)
                            if j25 is not None and n >= 3 and n % 2 == 1:
                                acc((n - 3) // 2)
                        sc.wait_ge(s_vv.h, VB + 2 + DS - 1)
                        sc.copy(hstate[:, c * DS + DS - 2:c * DS + DS - 1],
                                h_t[0][:, LT2 - 1:LT2])
                        s_ev.inc(sc.copy(hstate[:, c * DS + DS - 1:c * DS + DS],
                                         h_t[1][:, LT2 - 1:LT2]))
                        if j15 is not None:
                            gel(14)
                            gel(15)
                        if j25 is not None:
                            acc(7)
                    block.scalar(p2s)
                    assert s_ev.n == EvB + evtot, (s_ev.n, EvB, evtot)
                    GB = s_gp.n

                    def p2g(g_, cb=cb, lb=lb, GB=GB, VB=VB, bck=bck, j15=j15,
                            gelinc=gelinc, prev_vb=prev_dtv[0] - 1,
                            seq=seq, ldk=ldk):
                        g_.wait_ge(s_ld.h, bck)
                        g_.wait_ge(s_vv.h, VB + 1)
                        for n in range(DS):
                            nb = n % 2
                            if n >= 2:
                                g_.wait_ge(s_vv.h, VB + 2 + n - 2)
                            elif seq >= 1:
                                g_.wait_ge(s_vv.h, prev_vb + 16 + n)
                            s_gp.inc(g_.tensor_tensor(
                                dbx_t[nb], dtxs[cb],
                                bb_t[lb][:, n * LT2:(n + 1) * LT2], ALU.mult))
                        if j15 is not None:
                            g_.wait_ge(s_ld.h, p5ck[j15]["ld"])
                            for i5 in range(4):
                                g_.wait_ge(s_ev.h, gelinc(4 * i5 + 3))
                                sl = m_gam[:, i5 * 4 * LT5:(i5 + 1) * 4 * LT5
                                           ].rearrange("p (q l) -> p q l", q=4)
                                ins = g_.tensor_tensor(
                                    sl, sl,
                                    rep0(m_gb[:, i5 * LT5:(i5 + 1) * LT5], 4),
                                    ALU.mult)
                            s_gp.inc(ins)
                    block.gpsimd(p2g)
                    assert s_gp.n == GB + DS + (1 if j15 is not None else 0)
                    if j15 is not None:
                        p5ck[j15]["gate"] = GB + DS + 1
                    ck_pd[(lt, c)] = EvB + 1
                    if j15 is not None:
                        p5ck[j15]["gel14"] = EvB + 33
                        p5ck[j15]["gel15"] = EvB + 34
                    if j25 is not None:
                        p5ck[j25]["acc6"] = EvB + 24
                        p5ck[j25]["acc7"] = EvB + 26

                    def p2v(v, cb=cb, c=c, lt=lt, lb=lb, EvB=EvB, MB=MB,
                            bck=bck, daev=daev, hc_abs=hc_abs, GB=GB,
                            prev_hca=prev_hc[0], prev_gb=prev_gp[0]):
                        v.wait_ge(s_ev.h, EvB + 1)
                        v.wait_ge(s_ld.h, bck)
                        if prev_gb is not None:
                            v.wait_ge(s_gp.h, prev_gb + DS)
                        v.tensor_tensor(dt_f[cb], dt_f[cb], tA, ALU.add)
                        s_vv.inc(v.tensor_tensor(dtxs[cb], dt_f[cb], xs_t[cb],
                                                 ALU.mult))
                        for n in range(DS):
                            nb = n % 2
                            col = c * DS + n
                            if n >= 2:
                                v.wait_ge(s_mm.h, hc_abs[n - 2])
                            elif prev_hca is not None:
                                v.wait_ge(s_mm.h, prev_hca[14 + n])
                            v.wait_ge(s_gp.h, GB + n + 1)
                            v.wait_ge(s_ev.h, daev(n))
                            v.tensor_tensor_scan(h_t[nb], da_t[nb], dbx_t[nb],
                                                 hstate[:, col:col + 1], ALU.mult, ALU.add)
                            s_vv.inc(v.tensor_tensor(hc_t[nb], h_t[nb],
                                                     cb_t[lb][:, n * LT2:(n + 1) * LT2],
                                                     ALU.mult))
                    block.vector(p2v)
                    prev_gp[0] = GB
                    prev_dtv[0] = VB + 1
                    prev_hc[0] = hc_abs
                    ck_xs[(lt, c)] = (hc_abs[0], VB + 1)

                    if debug:
                        def dtdump(sync, c=c, l0=l0, VB=VB, EvB=EvB):
                            sync.wait_ge(s_vv.h, VB + 2)
                            s_ld.dinc(sync.dma_start(
                                out=dt_d[c * 128:(c + 1) * 128, l0:l0 + LT2],
                                in_=dt_f[c % 2]))
                        block.sync(dtdump)

                    def p2t(t, cb=cb, c=c, lt=lt, VB=VB, MB=MB, j15=j15,
                            j25=j25, gelinc=gelinc, accinc=accinc):
                        yp = py[c % 2]
                        if j25 is not None:
                            half5b = j25 // 16
                            b5b = j25 % 2
                        for n in range(DS):
                            nb = n % 2
                            if j15 is not None:
                                g = n
                                i5, jj5 = g // 4, g % 4
                                if g == 0:
                                    t.wait_ge(s_ld.h, max(p5ck[j15]["ld"],
                                                          wld1[j15 // 16]))
                                if g < 2:
                                    if j15 >= 1:
                                        t.wait_ge(s_ev.h, p5ck[j15 - 1][
                                            "gel14" if g == 0 else "gel15"])
                                else:
                                    t.wait_ge(s_ev.h, gelinc(g - 2))
                                for kk in range(NCM):
                                    ins = t.matmul(
                                        p5p1[g % 2],
                                        m_w1[:, (i5 * NCM + kk) * DE + jj5 * 128:
                                             (i5 * NCM + kk) * DE + (jj5 + 1) * 128],
                                        m_xn.rearrange("p (q l) -> p q l",
                                                       q=NCM)[:, kk],
                                        start=(kk == 0), stop=(kk == NCM - 1))
                                s_mm.inc(ins)
                            if j25 is not None and n % 2 == 1:
                                m = (n - 1) // 2
                                if m == 0:
                                    t.wait_ge(s_gp.h, p5ck[j25]["gate"])
                                    t.wait_ge(s_ld.h, wld2[half5b])
                                if m < 2:
                                    if j25 >= 1:
                                        t.wait_ge(s_ev.h, p5ck[j25 - 1][
                                            "acc6" if m == 0 else "acc7"])
                                else:
                                    t.wait_ge(s_ev.h, accinc(m - 2))
                                for q in range(16):
                                    ins = t.matmul(
                                        p5p2[m % 2],
                                        m_w2[:, q * DM + m * 128:q * DM + (m + 1) * 128],
                                        m_gam[:, q * LT5:(q + 1) * LT5],
                                        start=(q == 0),
                                        stop=(half5b == 0 and q == 15))
                                if half5b == 1:
                                    ins = t.matmul(p5p2[m % 2],
                                                   m_b2[:, m * 128:(m + 1) * 128],
                                                   g_sl[b5b], start=False, stop=True)
                                s_mm.inc(ins)
                            t.wait_ge(s_vv.h, VB + 2 + n)
                            if n == 0:
                                pk = (lt, c - 2) if c >= 2 else (lt - 1, NCI - 2 + c)
                                if pk in ck_y:
                                    t.wait_ge(s_vv.h, ck_y[pk])
                                for hh in range(2):
                                    t.matmul(yp[:, hh * 512:(hh + 1) * 512],
                                             diagD[:, c * 128:(c + 1) * 128],
                                             xs_t[cb][:, hh * 512:(hh + 1) * 512],
                                             start=True, stop=False)
                            for hh in range(2):
                                ins = t.matmul(yp[:, hh * 512:(hh + 1) * 512], ident_b,
                                               hc_t[nb][:, hh * 512:(hh + 1) * 512],
                                               start=False, stop=(n == DS - 1))
                            s_mm.inc(ins)
                    block.tensor(p2t)
                    assert s_mm.n == hc_abs[DS - 1], (s_mm.n, MB, hc_abs[DS - 1])
                    if j15 is not None:
                        p5ck[j15]["t15_end"] = MB + 31
                    if j25 is not None:
                        p5ck[j25]["t25_end"] = MB + 23
                    ymmk = hc_abs[DS - 1]

                    def yev(v, c=c, ymmk=ymmk, lt=lt):
                        v.wait_ge(s_mm.h, ymmk)
                        pk = (lt, c - 1) if c >= 1 else (lt - 1, NCI - 1)
                        if pk in ck_yst:
                            v.wait_ge(s_st.h, ck_yst[pk])
                        s_vv.inc(v.tensor_copy(y_t[c % 2], py[c % 2]))
                    block.vector(yev)
                    ck_y[(lt, c)] = s_vv.n
                    yevk = s_vv.n

                    def yst(sync, c=c, yevk=yevk, l0=l0):
                        sync.wait_ge(s_vv.h, yevk)
                        s_st.dinc(sync.dma_start(
                            out=y_d[c * 128:(c + 1) * 128, l0:l0 + LT2], in_=y_t[c % 2]))
                    block.sync(yst)
                    ck_yst[(lt, c)] = s_st.n

                    if j25 is not None:
                        def st5(sync, j25=j25, EvB=EvB):
                            sync.wait_ge(s_ev.h, EvB + 26)
                            dst = macc_d if j25 < 16 else macc2_d
                            l05 = (j25 % 16) * LT5
                            s_st.dinc(sync.dma_start(
                                out=dst.rearrange("(q p) l -> p q l",
                                                  p=128)[:, :, l05:l05 + LT5],
                                in_=m_acc3))
                        block.sync(st5)
                        p5ck[j25]["st"] = s_st.n

                ck_vend[lt] = s_vv.n
                ck_gend[lt] = s_gp.n

            esE = (4, 5, 6, 7)
            b5E, l05E = 1, 15 * LT5

            def ld5e(sync):
                s_mm.wait(sync, p5ck[30]["t15_end"])
                s_gp.wait(sync, p5ck[30]["gate"])
                s_ld.dinc(sync.dma_start(out=g_sl[b5E],
                                         in_=gatesv_d[:, l05E:l05E + LT5]))
                gslk = s_ld.n
                s_ld.dinc(sync.dma_start(
                    out=m_xn.rearrange("p (q l) -> p q l", q=NCM),
                    in_=xn_bf_d.rearrange("(q p) l -> p q l",
                                          p=128)[:, :, l05E:l05E + LT5]))
                sync.wait_ge(s_ld.h, gslk)
                for i, e in enumerate(esE):
                    s_ld.dinc(sync.dma_start(
                        out=m_gb[:, i * LT5:(i + 1) * LT5],
                        in_=brow(g_sl[b5E][e:e + 1, :])))
            block.sync(ld5e)
            ldE = s_ld.n
            MBe = s_mm.n
            EvE = s_ev.n

            def t15e(t):
                t.wait_ge(s_ld.h, max(ldE, wld1[1]))
                for g in range(16):
                    i5, jj5 = g // 4, g % 4
                    if g < 2:
                        t.wait_ge(s_ev.h, p5ck[30]["gel14" if g == 0 else "gel15"])
                    else:
                        t.wait_ge(s_ev.h, EvE + g - 1)
                    for kk in range(NCM):
                        ins = t.matmul(
                            p5p1[g % 2],
                            m_w1[:, (i5 * NCM + kk) * DE + jj5 * 128:
                                 (i5 * NCM + kk) * DE + (jj5 + 1) * 128],
                            m_xn.rearrange("p (q l) -> p q l", q=NCM)[:, kk],
                            start=(kk == 0), stop=(kk == NCM - 1))
                    s_mm.inc(ins)
            block.tensor(t15e)
            assert s_mm.n == MBe + 16

            def s15e(sc):
                for g in range(16):
                    sc.wait_ge(s_mm.h, MBe + g + 1)
                    if g < 2:
                        sc.wait_ge(s_mm.h, p5ck[30]["t25_end"])
                    e5 = esE[g // 4]
                    s_ev.inc(sc.activation(
                        m_gam[:, g * LT5:(g + 1) * LT5], p5p1[g % 2], AF.Gelu,
                        bias=p_moeb1[:, e5 * 4 + g % 4:e5 * 4 + g % 4 + 1]))
            block.scalar(s15e)
            assert s_ev.n == EvE + 16

            def gate5e(g_):
                g_.wait_ge(s_ld.h, ldE)
                for i5 in range(4):
                    g_.wait_ge(s_ev.h, EvE + 4 * i5 + 4)
                    sl = m_gam[:, i5 * 4 * LT5:(i5 + 1) * 4 * LT5].rearrange(
                        "p (q l) -> p q l", q=4)
                    ins = g_.tensor_tensor(
                        sl, sl, rep0(m_gb[:, i5 * LT5:(i5 + 1) * LT5], 4),
                        ALU.mult)
                s_gp.inc(ins)
            block.gpsimd(gate5e)
            gateE = s_gp.n
            MBe2 = s_mm.n
            EvE2 = s_ev.n

            def t25e(t):
                t.wait_ge(s_gp.h, gateE)
                t.wait_ge(s_ld.h, wld2[1])
                for m in range(NCM):
                    if m < 2:
                        t.wait_ge(s_ev.h, p5ck[30]["acc6" if m == 0 else "acc7"])
                    else:
                        t.wait_ge(s_ev.h, EvE2 + m - 1)
                    for q in range(16):
                        t.matmul(p5p2[m % 2],
                                 m_w2[:, q * DM + m * 128:q * DM + (m + 1) * 128],
                                 m_gam[:, q * LT5:(q + 1) * LT5],
                                 start=(q == 0), stop=False)
                    ins = t.matmul(p5p2[m % 2], m_b2[:, m * 128:(m + 1) * 128],
                                   g_sl[b5E], start=False, stop=True)
                    s_mm.inc(ins)
            block.tensor(t25e)
            assert s_mm.n == MBe2 + NCM

            def s25e(sc):
                for m in range(NCM):
                    sc.wait_ge(s_mm.h, MBe2 + m + 1)
                    if m == 0:
                        sc.wait_ge(s_st.h, p5ck[30]["st"])
                    s_ev.inc(sc.copy(m_acc3[:, m], p5p2[m % 2]))
            block.scalar(s25e)
            assert s_ev.n == EvE2 + NCM

            def st5e(sync):
                sync.wait_ge(s_ev.h, EvE2 + NCM)
                s_st.dinc(sync.dma_start(
                    out=macc2_d.rearrange("(q p) l -> p q l",
                                          p=128)[:, :, l05E:l05E + LT5],
                    in_=m_acc3))
            block.sync(st5e)

            barrier()

        LT3 = 512
        NT3 = L // LT3
        with contextlib.ExitStack() as p3:
            psb = lambda n_, sh, dt: p3.enter_context(nc.sbuf_tensor(n_, sh, dt))[:]
            wout_sb = psb("p3wo", [128, NCI * DM], BF16)
            y_sb = [psb(f"p3y{i}", [128, NCI * LT3], BF16) for i in range(2)]
            zs_sb = [psb(f"p3zs{i}", [128, NCI * LT3], BF16) for i in range(2)]
            xn_sb = [psb(f"p3xn{i}", [128, NCM * LT3], BF16) for i in range(2)]
            t_one = psb("p3t", [128, NCM * LT3], F32)
            t_f = [t_one, t_one]
            t_b = psb("p3tb", [128, NCM * LT3], BF16)
            sq_b = psb("p3sq", [128, NCM * LT3], BF16)
            hb_t = [psb(f"p3hb{i}", [128, NCM * LT3], BF16) for i in range(2)]
            mrow = psb("p3m", [1, LT3], F32)
            rrow = psb("p3r", [1, LT3], F32)
            srow = psb("p3s", [1, 4 * LT3], F32)
            m_b = psb("p3mb", [128, LT3], F32)
            r_b = psb("p3rb", [128, LT3], F32)
            pb = [p3.enter_context(nc.psum_tensor(f"p3ps{i}", [128, LT3], F32))[:]
                  for i in range(4)]
            ps_sum = p3.enter_context(nc.psum_tensor("p3psum", [1, LT3], F32))[:]
            ps_sq = p3.enter_context(nc.psum_tensor("p3psq", [1, LT3], F32))[:]

            def w_ld(sync):
                s_ld.dinc(sync.dma_start(
                    out=wout_sb.rearrange("p (c e) -> p c e", c=NCI), in_=woutT_b.rearrange("(c p) e -> p c e", p=128)))
            block.sync(w_ld)
            wldk = s_ld.n

            bank_ck = [0] * 4
            ck_in = [0] * NT3
            ck_stat = [0] * NT3
            ck_hst = [0] * NT3
            ck_hst_dma = [0] * NT3
            for ti in range(NT3):
                b = ti % 2
                l0 = ti * LT3

                def ld(sync, b=b, l0=l0, ti=ti):
                    if ti >= 2:
                        s_vv.wait(sync, ck_in[ti - 2])
                    s_ld.dinc(sync.dma_start(
                        out=y_sb[b].rearrange("p (c l) -> p c l", c=NCI),
                        in_=y_d.rearrange("(c p) l -> p c l", p=128)[:, :, l0:l0 + LT3]))
                    s_ld.dinc(sync.dma_start(
                        out=zs_sb[b].rearrange("p (c l) -> p c l", c=NCI),
                        in_=zsil_d.rearrange("(c p) l -> p c l", p=128)[:, :, l0:l0 + LT3]))
                    s_ld.dinc(sync.dma_start(
                        out=xn_sb[b].rearrange("p (c l) -> p c l", c=NCM),
                        in_=xn_bf_d.rearrange("(c p) l -> p c l", p=128)[:, :, l0:l0 + LT3]))
                block.sync(ld)
                ldk = s_ld.n

                def ysz(v, b=b, ldk=ldk):
                    v.wait_ge(s_ld.h, ldk)
                    s_vv.inc(v.tensor_tensor(y_sb[b], y_sb[b], zs_sb[b], ALU.mult))
                block.vector(ysz)
                yszk = s_vv.n

                MBo = s_mm.n
                VBo = s_vv.n

                def t_o(t, b=b, yszk=yszk, VBo=VBo):
                    t.wait_ge(s_vv.h, yszk)
                    t.wait_ge(s_ld.h, wldk)
                    for m in range(NCM):
                        bank = m % 4
                        if m >= 4:
                            t.wait_ge(s_vv.h, VBo + m - 4 + 1)
                        elif bank_ck[bank]:
                            t.wait_ge(s_vv.h, bank_ck[bank])
                        for k in range(NCI):
                            ins = t.matmul(
                                pb[bank], wout_sb[:, k * DM + m * 128:k * DM + (m + 1) * 128],
                                y_sb[b].rearrange("p (c l) -> p c l", c=NCI)[:, k],
                                start=(k == 0), stop=(k == NCI - 1))
                        s_mm.inc(ins)
                block.tensor(t_o)
                assert s_mm.n == MBo + NCM, ('s_mm', s_mm.n, MBo + NCM)

                def v_o(v, b=b, MBo=MBo, ti=ti):
                    if ti >= 1:
                        v.wait_ge(s_ev.h, ck_hst[ti - 1])
                    for m in range(NCM):
                        v.wait_ge(s_mm.h, MBo + m + 1)
                        s_vv.inc(v.tensor_tensor(
                            t_f[b].rearrange("p (c l) -> p c l", c=NCM)[:, m], pb[m % 4],
                            xn_sb[b].rearrange("p (c l) -> p c l", c=NCM)[:, m], ALU.add))
                block.vector(v_o)
                assert s_vv.n == VBo + NCM, ('s_vv', s_vv.n, VBo + NCM)
                for m in range(4, 8):
                    bank_ck[m % 4] = VBo + m + 1
                ck_in[ti] = s_vv.n
                resk = s_vv.n

                def cast(s, b=b, resk=resk, ti=ti):
                    s.wait_ge(s_vv.h, resk)
                    if ti >= 1:
                        s.wait_ge(s_mm.h, ck_stat[ti - 1])
                    s.copy(t_b, t_f[b])
                    s_ev.inc(s.activation(sq_b, t_f[b], AF.Square))
                block.scalar(cast)
                cak = s_ev.n

                def stat_mm(t, cak=cak):
                    t.wait_ge(s_ev.h, cak)
                    for c in range(NCM):
                        t.matmul(ps_sum, ones_b,
                                 t_b.rearrange("p (c l) -> p c l", c=NCM)[:, c],
                                 start=(c == 0), stop=(c == NCM - 1))
                    for c in range(NCM):
                        ins = t.matmul(ps_sq, ones_b,
                                       sq_b.rearrange("p (c l) -> p c l", c=NCM)[:, c],
                                       start=(c == 0), stop=(c == NCM - 1))
                    s_mm.inc(ins)
                block.tensor(stat_mm)
                ck_stat[ti] = s_mm.n
                stk = s_mm.n

                def stat_s(s, stk=stk):
                    s.wait_ge(s_mm.h, stk)
                    s.mul(mrow, ps_sum, 1.0 / DM)
                    s.activation(srow[0:1, 0 * LT0:1 * LT0], mrow, AF.Square)
                    s_ev.inc(s.mul(srow[0:1, 1 * LT0:2 * LT0], ps_sq, 1.0 / DM))
                block.scalar(stat_s)
                ssk = s_ev.n

                def stat_v(v, ssk=ssk):
                    v.wait_ge(s_ev.h, ssk)
                    v.tensor_tensor(srow[0:1, 2 * LT0:3 * LT0], srow[0:1, 1 * LT0:2 * LT0], srow[0:1, 0 * LT0:1 * LT0], ALU.subtract)
                    s_vv.inc(v.tensor_single_scalar(srow[0:1, 2 * LT0:3 * LT0], srow[0:1, 2 * LT0:3 * LT0],
                                                    1e-12, ALU.add))
                block.vector(stat_v)
                svk = s_vv.n

                def stat_s2(s, svk=svk):
                    s.wait_ge(s_vv.h, svk)
                    s_ev.inc(s.activation(srow[0:1, 3 * LT0:4 * LT0], srow[0:1, 2 * LT0:3 * LT0], AF.Sqrt))
                block.scalar(stat_s2)
                ss2k = s_ev.n

                def stat_v2(v, ss2k=ss2k):
                    v.wait_ge(s_ev.h, ss2k)
                    s_vv.inc(v.reciprocal(rrow, srow[0:1, 3 * LT0:4 * LT0]))
                block.vector(stat_v2)
                sv2k = s_vv.n

                def bcst(sync, sv2k=sv2k):
                    sync.wait_ge(s_vv.h, sv2k)
                    s_ld.dinc(sync.dma_start(out=m_b, in_=brow(mrow)))
                    s_ld.dinc(sync.dma_start(out=r_b, in_=brow(rrow)))
                block.sync(bcst)
                bck = s_ld.n

                def apv(v, b=b, bck=bck):
                    v.wait_ge(s_ld.h, bck)
                    t3 = t_f[b].rearrange("p (c l) -> p c l", c=NCM)
                    v.tensor_tensor(t3, t3, rep0(m_b, NCM), ALU.subtract)
                    s_vv.inc(v.tensor_tensor(t3, t3, rep0(r_b, NCM), ALU.mult))
                block.vector(apv)
                apk = s_vv.n

                def aff(s, b=b, apk=apk, ti=ti):
                    s.wait_ge(s_vv.h, apk)
                    if ti >= 2:
                        s.wait_ge(s_st.h, ck_hst_dma[ti - 2])
                    hb = hb_t[ti % 2]
                    for c in range(NCM):
                        ins = s.activation(hb.rearrange("p (c l) -> p c l", c=NCM)[:, c],
                                           t_f[b].rearrange("p (c l) -> p c l", c=NCM)[:, c],
                                           AF.Identity, bias=p_ln1b[:, c:c + 1],
                                           scale=p_ln1w[:, c:c + 1])
                    s_ev.inc(ins)
                block.scalar(aff)
                ck_hst[ti] = s_ev.n
                afk = s_ev.n

                def hst(sync, ti=ti, afk=afk, l0=l0):
                    sync.wait_ge(s_ev.h, afk)
                    s_st.dinc(sync.dma_start(
                        out=h_d.rearrange("(c p) l -> p c l", p=128)[:, :, l0:l0 + LT3],
                        in_=hb_t[ti % 2].rearrange("p (c l) -> p c l", c=NCM)))
                block.sync(hst)
                ck_hst_dma[ti] = s_st.n

            barrier()

        LT4 = 512
        NT4 = L // LT4
        with contextlib.ExitStack() as p4:
            psb = lambda n_, sh, dt: p4.enter_context(nc.sbuf_tensor(n_, sh, dt))[:]
            w1_sb = psb("p4w1", [128, NCM * DI], BF16)
            w2_sb = psb("p4w2", [128, NCI * DM], BF16)
            h_sb = [psb(f"p4h{i}", [128, NCM * LT4], BF16) for i in range(2)]
            f_sb = [psb(f"p4f{i}", [128, NCI * LT4], BF16) for i in range(2)]
            t4_one = psb("p4t", [128, NCM * LT4], F32)
            t_f = [t4_one, t4_one]
            t_b = psb("p4tb", [128, NCM * LT4], BF16)
            sq_b = psb("p4sq", [128, NCM * LT4], BF16)
            o4_one = psb("p4o", [128, NCM * LT4], F32)
            o_f = [o4_one, o4_one]
            mrow = psb("p4m", [1, LT4], F32)
            rrow = psb("p4r", [1, LT4], F32)
            srow = psb("p4s", [1, 4 * LT4], F32)
            m_b = psb("p4mb", [128, LT4], F32)
            r_b = psb("p4rb", [128, LT4], F32)
            pb = [p4.enter_context(nc.psum_tensor(f"p4ps{i}", [128, LT4], F32))[:]
                  for i in range(6)]
            ps_sum = p4.enter_context(nc.psum_tensor("p4psum", [1, LT4], F32))[:]
            ps_sq = p4.enter_context(nc.psum_tensor("p4psq", [1, LT4], F32))[:]

            def w_ld(sync):
                s_ld.dinc(sync.dma_start(
                    out=w1_sb.rearrange("p (c e) -> p c e", c=NCM), in_=ffw1T_b.rearrange("(c p) e -> p c e", p=128)))
                s_ld.dinc(sync.dma_start(
                    out=w2_sb.rearrange("p (c e) -> p c e", c=NCI), in_=ffw2T_b.rearrange("(c p) e -> p c e", p=128)))
            block.sync(w_ld)
            wldk = s_ld.n

            bank_ck = [(None, 0)] * 6
            ck_h_cons = [0] * NT4
            ck_f_cons = [0] * NT4
            ck_stat = [0] * NT4
            ck_ost = [0] * NT4
            ck_aff = [0] * NT4

            def bwait(t, bank):
                sem, cnt = bank_ck[bank]
                if sem is not None:
                    t.wait_ge(sem.h, cnt)

            for ti in range(NT4):
                b = ti % 2
                l0 = ti * LT4

                def ld(sync, b=b, l0=l0, ti=ti):
                    if ti >= 2:
                        s_vv.wait(sync, ck_h_cons[ti - 2])
                    s_ld.dinc(sync.dma_start(
                        out=h_sb[b].rearrange("p (c l) -> p c l", c=NCM),
                        in_=h_d.rearrange("(c p) l -> p c l", p=128)[:, :, l0:l0 + LT4]))
                block.sync(ld)
                ldk = s_ld.n

                MB1 = s_mm.n
                EvB1 = s_ev.n

                def t1(t, b=b, ldk=ldk, ti=ti, EvB1=EvB1):
                    t.wait_ge(s_ld.h, max(ldk, wldk))
                    if ti >= 2:
                        t.wait_ge(s_mm.h, ck_f_cons[ti - 2])
                    for m in range(NCI):
                        bank = m % 6
                        if m >= 6:
                            t.wait_ge(s_ev.h, EvB1 + m - 6 + 1)
                        else:
                            sem, cnt = bank_ck[bank]
                            if sem is not None:
                                t.wait_ge(sem.h, cnt)
                        for k in range(NCM):
                            ins = t.matmul(
                                pb[bank], w1_sb[:, k * DI + m * 128:k * DI + (m + 1) * 128],
                                h_sb[b].rearrange("p (c l) -> p c l", c=NCM)[:, k],
                                start=(k == 0), stop=(k == NCM - 1))
                        s_mm.inc(ins)
                block.tensor(t1)
                assert s_mm.n == MB1 + NCI, ('s_mm', s_mm.n, MB1 + NCI)

                def s1(s, b=b, MB1=MB1):
                    for m in range(NCI):
                        s.wait_ge(s_mm.h, MB1 + m + 1)
                        s_ev.inc(s.activation(
                            f_sb[b].rearrange("p (c l) -> p c l", c=NCI)[:, m],
                            pb[m % 6], AF.Prelu, bias=p_ffb1[:, m:m + 1], alpha=0.01))
                block.scalar(s1)
                assert s_ev.n == EvB1 + NCI, ('s_ev', s_ev.n, EvB1 + NCI)
                for m in range(10, 16):
                    bank_ck[m % 6] = (s_ev, EvB1 + m + 1)
                fk = s_ev.n

                MB2 = s_mm.n
                VB2 = s_vv.n

                def t2(t, b=b, fk=fk, VB2=VB2):
                    t.wait_ge(s_ev.h, fk)
                    for m in range(NCM):
                        bank = m % 6
                        if m >= 6:
                            t.wait_ge(s_vv.h, VB2 + m - 6 + 1)
                        else:
                            sem, cnt = bank_ck[bank]
                            if sem is not None:
                                t.wait_ge(sem.h, cnt)
                        for k in range(NCI):
                            ins = t.matmul(
                                pb[bank], w2_sb[:, k * DM + m * 128:k * DM + (m + 1) * 128],
                                f_sb[b].rearrange("p (c l) -> p c l", c=NCI)[:, k],
                                start=(k == 0), stop=(k == NCI - 1))
                        s_mm.inc(ins)
                block.tensor(t2)
                assert s_mm.n == MB2 + NCM, ('s_mm', s_mm.n, MB2 + NCM)
                ck_f_cons[ti] = s_mm.n

                def v2(v, b=b, MB2=MB2, ti=ti):
                    if ti >= 1:
                        v.wait_ge(s_ev.h, ck_aff[ti - 1])
                    for m in range(NCM):
                        v.wait_ge(s_mm.h, MB2 + m + 1)
                        s_vv.inc(v.tensor_tensor(
                            t_f[b].rearrange("p (c l) -> p c l", c=NCM)[:, m], pb[m % 6],
                            h_sb[b].rearrange("p (c l) -> p c l", c=NCM)[:, m], ALU.add))
                block.vector(v2)
                assert s_vv.n == VB2 + NCM, ('s_vv', s_vv.n, VB2 + NCM)
                for m in range(2, 8):
                    bank_ck[m % 6] = (s_vv, VB2 + m + 1)
                ck_h_cons[ti] = s_vv.n
                resk = s_vv.n

                def cast(s, b=b, resk=resk, ti=ti):
                    s.wait_ge(s_vv.h, resk)
                    if ti >= 1:
                        s.wait_ge(s_mm.h, ck_stat[ti - 1])
                    for c in range(NCM):
                        t3 = t_f[b].rearrange("p (c l) -> p c l", c=NCM)[:, c]
                        s.activation(t_b.rearrange("p (c l) -> p c l", c=NCM)[:, c],
                                     t3, AF.Identity, bias=p_ffb2[:, c:c + 1])
                        ins = s.activation(sq_b.rearrange("p (c l) -> p c l", c=NCM)[:, c],
                                           t3, AF.Square, bias=p_ffb2[:, c:c + 1])
                    s_ev.inc(ins)
                block.scalar(cast)
                cak = s_ev.n

                def stat_mm(t, cak=cak):
                    t.wait_ge(s_ev.h, cak)
                    for c in range(NCM):
                        t.matmul(ps_sum, ones_b,
                                 t_b.rearrange("p (c l) -> p c l", c=NCM)[:, c],
                                 start=(c == 0), stop=(c == NCM - 1))
                    for c in range(NCM):
                        ins = t.matmul(ps_sq, ones_b,
                                       sq_b.rearrange("p (c l) -> p c l", c=NCM)[:, c],
                                       start=(c == 0), stop=(c == NCM - 1))
                    s_mm.inc(ins)
                block.tensor(stat_mm)
                ck_stat[ti] = s_mm.n
                stk = s_mm.n

                def stat_s(s, stk=stk):
                    s.wait_ge(s_mm.h, stk)
                    s.mul(mrow, ps_sum, 1.0 / DM)
                    s.activation(srow[0:1, 0 * LT0:1 * LT0], mrow, AF.Square)
                    s_ev.inc(s.mul(srow[0:1, 1 * LT0:2 * LT0], ps_sq, 1.0 / DM))
                block.scalar(stat_s)
                ssk = s_ev.n

                def stat_v(v, ssk=ssk):
                    v.wait_ge(s_ev.h, ssk)
                    v.tensor_tensor(srow[0:1, 2 * LT0:3 * LT0], srow[0:1, 1 * LT0:2 * LT0], srow[0:1, 0 * LT0:1 * LT0], ALU.subtract)
                    s_vv.inc(v.tensor_single_scalar(srow[0:1, 2 * LT0:3 * LT0], srow[0:1, 2 * LT0:3 * LT0],
                                                    1e-12, ALU.add))
                block.vector(stat_v)
                svk = s_vv.n

                def stat_s2(s, svk=svk):
                    s.wait_ge(s_vv.h, svk)
                    s_ev.inc(s.activation(srow[0:1, 3 * LT0:4 * LT0], srow[0:1, 2 * LT0:3 * LT0], AF.Sqrt))
                block.scalar(stat_s2)
                ss2k = s_ev.n

                def stat_v2(v, ss2k=ss2k):
                    v.wait_ge(s_ev.h, ss2k)
                    s_vv.inc(v.reciprocal(rrow, srow[0:1, 3 * LT0:4 * LT0]))
                block.vector(stat_v2)
                sv2k = s_vv.n

                def bcst(sync, sv2k=sv2k):
                    sync.wait_ge(s_vv.h, sv2k)
                    s_ld.dinc(sync.dma_start(out=m_b, in_=brow(mrow)))
                    s_ld.dinc(sync.dma_start(out=r_b, in_=brow(rrow)))
                block.sync(bcst)
                bck = s_ld.n

                def apv(v, b=b, bck=bck):
                    v.wait_ge(s_ld.h, bck)
                    t3 = t_f[b].rearrange("p (c l) -> p c l", c=NCM)
                    for c in range(NCM):
                        v.scalar_tensor_tensor(t3[:, c], t3[:, c], p_ffb2[:, c:c + 1],
                                               m_b, ALU.add, ALU.subtract)
                    s_vv.inc(v.tensor_tensor(t3, t3, rep0(r_b, NCM), ALU.mult))
                block.vector(apv)
                apk = s_vv.n

                def aff(s, b=b, apk=apk, ti=ti):
                    s.wait_ge(s_vv.h, apk)
                    if ti >= 1:
                        s.wait_ge(s_st.h, ck_ost[ti - 1])
                    of = o_f[ti % 2]
                    for c in range(NCM):
                        ins = s.activation(of.rearrange("p (c l) -> p c l", c=NCM)[:, c],
                                           t_f[b].rearrange("p (c l) -> p c l", c=NCM)[:, c],
                                           AF.Identity, bias=p_ln2b[:, c:c + 1],
                                           scale=p_ln2w[:, c:c + 1])
                    s_ev.inc(ins)
                block.scalar(aff)
                afk = s_ev.n
                ck_aff[ti] = afk

                def ost(sync, ti=ti, afk=afk, l0=l0):
                    sync.wait_ge(s_ev.h, afk)
                    s_st.dinc(sync.dma_start(
                        out=attn_d.rearrange("(c p) l -> p c l", p=128)[:, :, l0:l0 + LT4],
                        in_=o_f[ti % 2].rearrange("p (c l) -> p c l", c=NCM)))
                block.sync(ost)
                ck_ost[ti] = s_st.n

            barrier()

        LT6 = 512
        NT6 = L // LT6
        with contextlib.ExitStack() as p6:
            psb = lambda n_, sh, dt: p6.enter_context(nc.sbuf_tensor(n_, sh, dt))[:]
            x_sb = [psb(f"p6x{i}", [128, NCM * LT6], F32) for i in range(2)]
            at_sb = [psb(f"p6at{i}", [128, NCM * LT6], F32) for i in range(2)]
            mo_sb = [psb(f"p6mo{i}", [128, NCM * LT6], BF16) for i in range(2)]
            mo2_sb = [psb(f"p6mo2{i}", [128, NCM * LT6], BF16) for i in range(2)]
            o_sb = [psb(f"p6o{i}", [128, NCM * LT6], F32) for i in range(2)]

            ck_o = [0] * NT6
            ck_ost6 = [0] * NT6
            for ti in range(NT6):
                b = ti % 2
                l0 = ti * LT6

                def ld(sync, b=b, l0=l0, ti=ti):
                    if ti >= 2:
                        s_vv.wait(sync, ck_o[ti - 2])
                    for dst, src in ((x_sb[b], x_fm), (at_sb[b], attn_d),
                                     (mo_sb[b], macc_d), (mo2_sb[b], macc2_d)):
                        s_ld.dinc(sync.dma_start(
                            out=dst.rearrange("p (c l) -> p c l", c=NCM),
                            in_=src.rearrange("(c p) l -> p c l", p=128)[:, :, l0:l0 + LT6]))
                block.sync(ld)
                ldk = s_ld.n

                def vsum(v, b=b, ldk=ldk, ti=ti):
                    v.wait_ge(s_ld.h, ldk)
                    if ti >= 2:
                        v.wait_ge(s_st.h, ck_ost6[ti - 2])
                    v.tensor_tensor(o_sb[b], x_sb[b], at_sb[b], ALU.add)
                    v.tensor_tensor(at_sb[b], mo_sb[b], mo2_sb[b], ALU.add)
                    s_vv.inc(v.tensor_tensor(o_sb[b], o_sb[b], at_sb[b], ALU.add))
                block.vector(vsum)
                ck_o[ti] = s_vv.n
                vk = s_vv.n

                def ost(sync, b=b, vk=vk, l0=l0):
                    sync.wait_ge(s_vv.h, vk)
                    s_st.dinc(sync.dma_start(
                        out=out_fm.rearrange("(c p) l -> p c l", p=128)[:, :, l0:l0 + LT6],
                        in_=o_sb[b].rearrange("p (c l) -> p c l", c=NCM)))
                block.sync(ost)
                ck_ost6[ti] = s_st.n

            def fin(sync):
                s_st.wait(sync)
                s_ld.wait(sync)
            block.sync(fin)

    return nc


def _relay(a, nchunks, inner):
    a = np.asarray(a)
    return np.ascontiguousarray(
        a.reshape(nchunks, 128, inner).transpose(1, 0, 2).reshape(128, nchunks * inner))


def _moeb1_relay(mb1):
    out = np.zeros((128, NE * 4), np.float32)
    for e in range(NE):
        for j in range(4):
            out[:, e * 4 + j] = mb1[e, j * 128:(j + 1) * 128]
    return out


def make_shared(inputs):
    f = lambda k: np.ascontiguousarray(np.asarray(inputs[k], dtype=np.float32))
    return dict(
        w_inT=np.ascontiguousarray(f("L0_attn_in_proj_w").T),
        conv_rel=_relay(f("L0_attn_conv_w"), NCI, DC),
        convb_rel=_relay(f("L0_attn_conv_b")[:, None], NCI, 1),
        xprojT=np.ascontiguousarray(f("L0_attn_x_proj_w").T),
        dtprojT=np.ascontiguousarray(f("L0_attn_dt_proj_w").T),
        dtb_rel=_relay(f("L0_attn_dt_proj_b")[:, None], NCI, 1),
        A_rel=_relay(f("L0_attn_A_log"), NCI, DS),
        D_rel=_relay(f("L0_attn_D")[:, None], NCI, 1),
        woutT=np.ascontiguousarray(f("L0_attn_out_proj_w").T),
        ln1w_rel=_relay(f("L0_attn_ln1_w")[:, None], NCM, 1),
        ln1b_rel=_relay(f("L0_attn_ln1_b")[:, None], NCM, 1),
        ffw1T=np.ascontiguousarray(f("L0_attn_ffn_w1").T),
        ffb1_rel=_relay(f("L0_attn_ffn_b1")[:, None], NCI, 1),
        ffw2T=np.ascontiguousarray(f("L0_attn_ffn_w2").T),
        ffb2_rel=_relay(f("L0_attn_ffn_b2")[:, None], NCM, 1),
        ln2w_rel=_relay(f("L0_attn_ln2_w")[:, None], NCM, 1),
        ln2b_rel=_relay(f("L0_attn_ln2_b")[:, None], NCM, 1),
        moe_w1=f("L0_ffn_w1"),
        moeb1_rel=_moeb1_relay(f("L0_ffn_b1")),
        moe_w2=f("L0_ffn_w2"),
        moe_b2=f("L0_ffn_b2"),
        centT=np.ascontiguousarray(f("L0_ffn_centroids").T),
        wrouteT=np.ascontiguousarray(f("L0_ffn_w_route").T),
    )


def kernel(**inputs):
    x = np.ascontiguousarray(np.asarray(inputs["L0_attn_x"], dtype=np.float32))
    shared = make_shared(inputs)
    nc = build(debug=DEBUG)
    in_maps = []
    for b in range(8):
        m = dict(shared)
        m["x_fm"] = np.ascontiguousarray(x[b].T)
        in_maps.append(m)
    res = run_bass_kernel_spmd(nc, in_maps, core_ids=list(range(8)))
    out = np.stack([np.ascontiguousarray(np.asarray(r["out_fm"]).T)
                    for r in res.results])
    return out.astype(np.float32)


# version 09
# speedup vs baseline: 1.1008x; 1.1008x over previous
import contextlib
import numpy as np
import concourse.bass as bass
import concourse.mybir as mybir
from concourse.bass_utils import run_bass_kernel_spmd

F32 = mybir.dt.float32
BF16 = mybir.dt.bfloat16
AF = mybir.ActivationFunctionType
ALU = mybir.AluOpType
AX = mybir.AxisListType

L = 4096
DM = 1024
DI = 2048
DS = 16
DC = 4
DTR = 64
NE = 8
DE = 512
NCM = DM // 128
NCI = DI // 128

DEBUG = False


class Sem:
    def __init__(self, ctx, nc, name):
        self.h = ctx.enter_context(nc.semaphore(name))
        self.n = 0

    def inc(self, instr, k=1):
        instr.then_inc(self.h, k)
        self.n += k
        return self.n

    def dinc(self, instr):
        return self.inc(instr, 16)

    def wait(self, eng, val=None):
        eng.wait_ge(self.h, self.n if val is None else val)


def brow(ap_row, parts=128):
    return bass.AP(ap_row.tensor, ap_row.offset,
                   [ap_row.ap[0], [0, parts], ap_row.ap[1]])


def rep0(ap2d, reps):
    return bass.AP(ap2d.tensor, ap2d.offset,
                   [ap2d.ap[0], [0, reps], ap2d.ap[1]])


def grp0(ap2d, inner):
    return bass.AP(ap2d.tensor, ap2d.offset,
                   [ap2d.ap[0], ap2d.ap[1], [0, inner]])


def build(debug=False):
    nc = bass.Bass("TRN2", target_bir_lowering=False, debug=False)

    EI = lambda name, shape: nc.dram_tensor(name, shape, F32, kind="ExternalInput").ap()
    SCR = lambda name, shape, dt: nc.dram_tensor(
        name, shape, dt, kind=("ExternalOutput" if debug else "Internal")).ap()

    x_fm = EI("x_fm", [DM, L])
    w_inT = EI("w_inT", [DM, 2 * DI])
    conv_rel = EI("conv_rel", [128, NCI * DC])
    convb_rel = EI("convb_rel", [128, NCI])
    xprojT = EI("xprojT", [DI, 96])
    dtprojT = EI("dtprojT", [DTR, DI])
    dtb_rel = EI("dtb_rel", [128, NCI])
    A_rel = EI("A_rel", [128, NCI * DS])
    D_rel = EI("D_rel", [128, NCI])
    woutT = EI("woutT", [DI, DM])
    ln1w_rel = EI("ln1w_rel", [128, NCM])
    ln1b_rel = EI("ln1b_rel", [128, NCM])
    ffw1T = EI("ffw1T", [DM, DI])
    ffb1_rel = EI("ffb1_rel", [128, NCI])
    ffw2T = EI("ffw2T", [DI, DM])
    ffb2_rel = EI("ffb2_rel", [128, NCM])
    ln2w_rel = EI("ln2w_rel", [128, NCM])
    ln2b_rel = EI("ln2b_rel", [128, NCM])
    moe_w1 = EI("moe_w1", [NE, DM, DE])
    moeb1_rel = EI("moeb1_rel", [128, NE * 4])
    moe_w2 = EI("moe_w2", [NE, DE, DM])
    moe_b2 = EI("moe_b2", [NE, DM])
    centT = EI("centT", [DM, NE])
    wrouteT = EI("wrouteT", [DM, NE])

    out_fm = nc.dram_tensor("out_fm", [DM, L], F32, kind="ExternalOutput").ap()

    xn_bf_d = SCR("xn_bf_d", [DM, L], BF16)
    predt_d = SCR("predt_d", [DI, L], F32)
    xsc_d = SCR("xsc_d", [DI, L], BF16)
    zsil_d = SCR("zsil_d", [DI, L], BF16)
    y_d = SCR("y_d", [DI, L], BF16)
    h_d = SCR("h_d", [DM, L], BF16)
    attn_d = SCR("attn_d", [DM, L], BF16)
    macc_d = nc.dram_tensor("macc_d", [DM, L], BF16, kind="Internal").ap()
    macc2_d = nc.dram_tensor("macc2_d", [DM, L], BF16, kind="Internal").ap()
    gatesv_d = nc.dram_tensor("gatesv_d", [NE, L], BF16, kind="Internal").ap()
    gates_d = SCR("gates_d", [128, 256], F32) if debug else None
    bcm_d = SCR("bcm_d", [96, L], mybir.dt.bfloat16) if debug else None
    dt_d = SCR("dt_d", [DI, L], F32) if debug else None
    hst_d = SCR("hst_d", [128, NCI * DS], F32) if debug else None
    lg_dbg = SCR("lg_dbg", [128, 256], F32) if debug else None
    gs_dbg0 = SCR("gs_dbg0", [128, 256], F32) if debug else None
    gs_dbg1 = SCR("gs_dbg1", [128, 256], F32) if debug else None
    gs_dbg2 = SCR("gs_dbg2", [128, 256], F32) if debug else None

    DB = lambda name, shape: nc.dram_tensor(name, shape, BF16, kind="Internal").ap()
    w_inT_b = DB("w_inT_b", [DM, 2 * DI])
    xprojT_b = DB("xprojT_b", [DI, 96])
    dtprojT_b = DB("dtprojT_b", [DTR, DI])
    woutT_b = DB("woutT_b", [DI, DM])
    ffw1T_b = DB("ffw1T_b", [DM, DI])
    ffw2T_b = DB("ffw2T_b", [DI, DM])
    moe_w1_b = DB("moe_w1_b", [NE, DM, DE])
    moe_w2_b = DB("moe_w2_b", [NE, DE, DM])
    moe_b2_b = DB("moe_b2_b", [NE, DM])

    ctx = contextlib.ExitStack()
    with ctx:
        sb = lambda name, shape, dt: ctx.enter_context(nc.sbuf_tensor(name, shape, dt))[:]
        ident_b = sb("ident_b", [128, 128], BF16)
        ones_b = sb("ones_b", [128, 1], BF16)
        diagD = sb("diagD", [128, NCI * 128], BF16)
        hstate = sb("hstate", [128, NCI * DS], F32)
        bcm_bf = sb("bcm_bf", [96, L], BF16)
        p_convb = sb("p_convb", [128, NCI], F32)
        p_dtb = sb("p_dtb", [128, NCI], F32)
        p_D = sb("p_D", [128, NCI], F32)
        p_ln1w = sb("p_ln1w", [128, NCM], F32)
        p_ln1b = sb("p_ln1b", [128, NCM], F32)
        p_ln2w = sb("p_ln2w", [128, NCM], F32)
        p_ln2b = sb("p_ln2b", [128, NCM], F32)
        p_ffb1 = sb("p_ffb1", [128, NCI], F32)
        p_ffb2 = sb("p_ffb2", [128, NCM], F32)
        p_moeb1 = sb("p_moeb1", [128, NE * 4], F32)
        p_invc = sb("p_invc", [NE, 1], F32)
        arel_sb = sb("arel_sb", [128, NCI * DS], F32)
        A_neg = sb("A_neg", [128, NCI * DS], F32)

        s_ld = Sem(ctx, nc, "s_ld")
        s_st = Sem(ctx, nc, "s_st")
        s_mm = Sem(ctx, nc, "s_mm")
        s_ev = Sem(ctx, nc, "s_ev")
        s_vv = Sem(ctx, nc, "s_vv")
        s_gp = Sem(ctx, nc, "s_gp")
        s_bar = Sem(ctx, nc, "s_bar")

        block = ctx.enter_context(nc.Block())
        ENGS = ("sync", "tensor", "scalar", "vector", "gpsimd")

        def barrier():
            target = s_bar.n + len(ENGS)
            for name in ENGS:
                def _sec(eng):
                    for s in (s_ld, s_st, s_mm, s_ev, s_vv, s_gp):
                        s.wait(eng)
                    eng.sem_inc(s_bar.h, 1)
                    eng.wait_ge(s_bar.h, target)
                getattr(block, name)(_sec)
            s_bar.n = target

        def st_sync(sync):
            for dst, src in ((p_convb, convb_rel), (p_dtb, dtb_rel), (p_D, D_rel),
                             (p_ln1w, ln1w_rel), (p_ln1b, ln1b_rel),
                             (p_ln2w, ln2w_rel), (p_ln2b, ln2b_rel),
                             (p_ffb1, ffb1_rel), (p_ffb2, ffb2_rel),
                             (p_moeb1, moeb1_rel),
                             (arel_sb, A_rel)):
                s_ld.dinc(sync.dma_start(out=dst, in_=src))
        block.sync(st_sync)
        ld0 = s_ld.n

        def st_gp(g):
            for ident in (ident_b,):
                g.memset(ident, 0.0)
                g.affine_select(out=ident, in_=ident, compare_op=ALU.not_equal,
                                fill=1.0, base=0, pattern=[[-1, 128]],
                                channel_multiplier=1)
            g.memset(ones_b, 1.0)
            s_gp.inc(g.memset(hstate, 0.0))
        block.gpsimd(st_gp)
        gp0 = s_gp.n

        def st_s(s):
            s.wait_ge(s_ld.h, ld0)
            s_ev.inc(s.activation(A_neg, arel_sb, AF.Exp))
        block.scalar(st_s)
        ev0 = s_ev.n

        def st_v(v):
            v.wait_ge(s_ev.h, ev0)
            v.wait_ge(s_gp.h, gp0)
            v.tensor_single_scalar(A_neg, A_neg, -1.0, ALU.mult)
            for c in range(NCI):
                ins = v.tensor_scalar(diagD[:, c * 128:(c + 1) * 128], ident_b,
                                      p_D[:, c:c + 1], None, ALU.mult)
            s_vv.inc(ins)
        block.vector(st_v)

        barrier()

        LT0 = 512
        NT0 = L // LT0
        with contextlib.ExitStack() as p0:
            psb = lambda n_, sh, dt: p0.enter_context(nc.sbuf_tensor(n_, sh, dt))[:]
            x_t = [psb(f"p0x{i}", [128, NCM * LT0], F32) for i in range(2)]
            xb_t = [psb(f"p0xb{i}", [128, NCM * LT0], BF16) for i in range(2)]
            sq_t = [psb(f"p0sq{i}", [128, NCM * LT0], BF16) for i in range(2)]
            xn_t = [psb(f"p0xn{i}", [128, NCM * LT0], BF16) for i in range(2)]
            srow = psb("p0srow", [1, 8 * LT0], F32)
            mrow = psb("p0m", [1, LT0], F32)
            rrow = psb("p0r", [1, LT0], F32)
            ixrow = psb("p0ix", [1, LT0], F32)
            m_b = psb("p0mb", [128, LT0], F32)
            r_b = psb("p0rb", [128, LT0], F32)
            ix_b = psb("p0ixb", [NE, LT0], F32)
            logits = psb("p0lg", [NE, L], F32)
            cstage = psb("p0cstg", [128, NCM * NE], F32)
            rstage = psb("p0rstg", [128, NCM * NE], F32)
            csq = psb("p0csq", [128, NCM * NE], F32)
            ones_f = psb("p0onesf", [128, 1], F32)
            cn_row = psb("p0cn", [NE, 1], F32)
            lg_tm = psb("p0lgtm", [128, 256], F32)
            ident_f = psb("p0identf", [128, 128], F32)
            gt_tm = psb("p0gttm", [128, 256], F32)
            gs = [psb(f"p0gs{i}", [128, 256], F32) for i in range(3)]
            gates_bf = psb("p0gates", [NE, L], BF16)

            x3 = lambda i: x_t[i].rearrange("p (c l) -> p c l", c=NCM)
            xb3 = lambda i: xb_t[i].rearrange("p (c l) -> p c l", c=NCM)
            sq3 = lambda i: sq_t[i].rearrange("p (c l) -> p c l", c=NCM)
            xn3 = lambda i: xn_t[i].rearrange("p (c l) -> p c l", c=NCM)

            with contextlib.ExitStack() as p0a:
                pps = lambda n_, sh: p0a.enter_context(nc.psum_tensor(n_, sh, F32))[:]
                ps_sum = pps("p0psum", [1, LT0])
                ps_sq = pps("p0psq", [1, LT0])
                ps_cos = pps("p0pcos", [NE, LT0])
                ps_rt = pps("p0prt", [NE, LT0])
                ps_cn = pps("p0pcn", [NE, 1])

                def c_sync(sync):
                    s_ld.dinc(sync.dma_start(
                        out=cstage.rearrange("p (c e) -> p c e", e=NE),
                        in_=centT.rearrange("(c p) e -> p c e", p=128)))
                    s_ld.dinc(sync.dma_start(
                        out=rstage.rearrange("p (c e) -> p c e", e=NE),
                        in_=wrouteT.rearrange("(c p) e -> p c e", p=128)))
                block.sync(c_sync)
                ldc = s_ld.n

                def c_gp(g):
                    g.memset(ones_f, 1.0)
                    g.memset(ident_f, 0.0)
                    s_gp.inc(g.affine_select(
                        out=ident_f, in_=ident_f, compare_op=ALU.not_equal,
                        fill=1.0, base=0, pattern=[[-1, 128]],
                        channel_multiplier=1))
                block.gpsimd(c_gp)
                gpc = s_gp.n

                def c_s(s):
                    s.wait_ge(s_ld.h, ldc)
                    s_ev.inc(s.activation(csq, cstage, AF.Square))
                block.scalar(c_s)
                evc = s_ev.n

                def c_t(t):
                    t.wait_ge(s_ev.h, evc)
                    t.wait_ge(s_gp.h, gpc)
                    for c in range(NCM):
                        ins = t.matmul(ps_cn, csq[:, c * NE:(c + 1) * NE], ones_f,
                                       start=(c == 0), stop=(c == NCM - 1))
                    s_mm.inc(ins)
                block.tensor(c_t)
                mmc = s_mm.n

                def c_s2(s):
                    s.wait_ge(s_mm.h, mmc)
                    s_ev.inc(s.activation(cn_row, ps_cn, AF.Sqrt))
                block.scalar(c_s2)
                evc2 = s_ev.n

                def c_v(v):
                    v.wait_ge(s_ev.h, evc2)
                    v.tensor_single_scalar(cn_row, cn_row, 1e-12, ALU.max)
                    s_vv.inc(v.reciprocal(p_invc, cn_row))
                block.vector(c_v)

                ck_xread = [0] * NT0
                ck_bfread = [0] * NT0
                ck_xn_cons = [0] * NT0
                ck_stat = [0] * NT0
                ck_logit = [0] * NT0
                ck_xnst = [0] * NT0

                for ti in range(NT0):
                    b = ti % 2
                    l0 = ti * LT0

                    def ld(sync, b=b, l0=l0, ti=ti):
                        if ti >= 2:
                            s_vv.wait(sync, ck_xread[ti - 2])
                            s_mm.wait(sync, ck_xn_cons[ti - 2])
                        s_ld.dinc(sync.dma_start(
                            out=x3(b),
                            in_=x_fm.rearrange("(c p) l -> p c l", p=128)[:, :, l0:l0 + LT0]))
                    block.sync(ld)
                    ldk = s_ld.n

                    def s1(s, b=b, ldk=ldk, ti=ti):
                        s.wait_ge(s_ld.h, ldk)
                        if ti >= 2:
                            s.wait_ge(s_mm.h, ck_bfread[ti - 2])
                        s.copy(xb_t[b], x_t[b])
                        s_ev.inc(s.activation(sq_t[b], x_t[b], AF.Square))
                    block.scalar(s1)
                    ev1 = s_ev.n

                    def mm(t, b=b, ev1=ev1, ti=ti):
                        t.wait_ge(s_ev.h, ev1)
                        for c in range(NCM):
                            t.matmul(ps_sum, ones_b, xb3(b)[:, c],
                                     start=(c == 0), stop=(c == NCM - 1))
                        for c in range(NCM):
                            ins = t.matmul(ps_sq, ones_b, sq3(b)[:, c],
                                           start=(c == 0), stop=(c == NCM - 1))
                        s_mm.inc(ins)
                    block.tensor(mm)
                    mmk = s_mm.n
                    ck_bfread[ti] = mmk

                    def s2(s, mmk=mmk):
                        s.wait_ge(s_mm.h, mmk)
                        s.mul(mrow, ps_sum, 1.0 / DM)
                        s.activation(srow[0:1, 0 * LT0:1 * LT0], mrow, AF.Square)
                        s_ev.inc(s.mul(srow[0:1, 1 * LT0:2 * LT0], ps_sq, 1.0 / DM))
                    block.scalar(s2)
                    ev2 = s_ev.n

                    def v1(v, ev2=ev2):
                        v.wait_ge(s_ev.h, ev2)
                        v.tensor_tensor(srow[0:1, 2 * LT0:3 * LT0],
                                        srow[0:1, 1 * LT0:2 * LT0],
                                        srow[0:1, 0 * LT0:1 * LT0], ALU.subtract)
                        s_vv.inc(v.tensor_single_scalar(
                            srow[0:1, 3 * LT0:4 * LT0],
                            srow[0:1, 2 * LT0:3 * LT0], 1e-5, ALU.add))
                    block.vector(v1)
                    vv1 = s_vv.n

                    def s3(s, vv1=vv1):
                        s.wait_ge(s_vv.h, vv1)
                        s_ev.inc(s.activation(srow[0:1, 4 * LT0:5 * LT0],
                                              srow[0:1, 3 * LT0:4 * LT0], AF.Sqrt))
                    block.scalar(s3)
                    ev3 = s_ev.n

                    def v2(v, ev3=ev3):
                        v.wait_ge(s_ev.h, ev3)
                        v.reciprocal(rrow, srow[0:1, 4 * LT0:5 * LT0])
                        v.tensor_tensor(srow[0:1, 5 * LT0:6 * LT0],
                                        srow[0:1, 2 * LT0:3 * LT0], rrow, ALU.mult)
                        v.tensor_tensor(srow[0:1, 5 * LT0:6 * LT0],
                                        srow[0:1, 5 * LT0:6 * LT0], rrow, ALU.mult)
                        s_vv.inc(v.tensor_single_scalar(
                            srow[0:1, 5 * LT0:6 * LT0],
                            srow[0:1, 5 * LT0:6 * LT0], float(DM), ALU.mult))
                    block.vector(v2)
                    vv2 = s_vv.n
                    ck_stat[ti] = vv2

                    def s4(s, vv2=vv2):
                        s.wait_ge(s_vv.h, vv2)
                        s_ev.inc(s.activation(srow[0:1, 6 * LT0:7 * LT0],
                                              srow[0:1, 5 * LT0:6 * LT0], AF.Sqrt))
                    block.scalar(s4)
                    ev4 = s_ev.n

                    def v2b(v, ev4=ev4):
                        v.wait_ge(s_ev.h, ev4)
                        v.tensor_single_scalar(srow[0:1, 6 * LT0:7 * LT0],
                                               srow[0:1, 6 * LT0:7 * LT0],
                                               1e-12, ALU.max)
                        s_vv.inc(v.reciprocal(ixrow, srow[0:1, 6 * LT0:7 * LT0]))
                    block.vector(v2b)
                    vv2b = s_vv.n

                    def bc(sync, vv2b=vv2b):
                        sync.wait_ge(s_vv.h, vv2b)
                        s_ld.dinc(sync.dma_start(out=m_b, in_=brow(mrow)))
                        s_ld.dinc(sync.dma_start(out=r_b, in_=brow(rrow)))
                        s_ld.dinc(sync.dma_start(out=ix_b, in_=brow(ixrow, NE)))
                    block.sync(bc)
                    bck = s_ld.n

                    def v3(v, b=b, bck=bck, ti=ti):
                        v.wait_ge(s_ld.h, bck)
                        if ti >= 2:
                            v.wait_ge(s_st.h, ck_xnst[ti - 2])
                            v.wait_ge(s_mm.h, ck_xn_cons[ti - 2])
                        v.tensor_tensor(x3(b), x3(b), rep0(m_b, NCM), ALU.subtract)
                        v.tensor_tensor(x3(b), x3(b), rep0(r_b, NCM), ALU.mult)
                        s_vv.inc(v.tensor_copy(xn_t[b], x_t[b]))
                    block.vector(v3)
                    xnk = s_vv.n
                    ck_xread[ti] = xnk

                    def mm2(t, b=b, xnk=xnk, ti=ti):
                        t.wait_ge(s_vv.h, xnk)
                        if ti >= 1:
                            t.wait_ge(s_vv.h, ck_logit[ti - 1])
                        for c in range(NCM):
                            t.matmul(ps_cos, cstage[:, c * NE:(c + 1) * NE], x3(b)[:, c],
                                     start=(c == 0), stop=(c == NCM - 1))
                        for c in range(NCM):
                            ins = t.matmul(ps_rt, rstage[:, c * NE:(c + 1) * NE],
                                           x3(b)[:, c], start=(c == 0), stop=(c == NCM - 1))
                        s_mm.inc(ins)
                    block.tensor(mm2)
                    mm2k = s_mm.n
                    ck_xn_cons[ti] = mm2k

                    def v4(v, mm2k=mm2k, l0=l0):
                        v.wait_ge(s_mm.h, mm2k)
                        lg = logits[:, l0:l0 + LT0]
                        v.tensor_scalar(ps_cos, ps_cos, p_invc, None, ALU.mult)
                        v.tensor_tensor(lg, ps_cos, ix_b, ALU.mult)
                        s_vv.inc(v.tensor_tensor(lg, lg, ps_rt, ALU.add))
                    block.vector(v4)
                    ck_logit[ti] = s_vv.n

                    def stx(sync, b=b, l0=l0, ti=ti):
                        s_vv.wait(sync, ck_xread[ti])
                        s_st.dinc(sync.dma_start(
                            out=xn_bf_d.rearrange("(c p) l -> p c l", p=128)[:, :, l0:l0 + LT0],
                            in_=xn3(b)))
                    block.sync(stx)
                    ck_xnst[ti] = s_st.n

            with contextlib.ExitStack() as p0b:
                ps_tp = [p0b.enter_context(nc.psum_tensor(f"p0tp{i}", [128, 512], F32))[:][:, 0:8]
                         for i in range(2)]
                ps_tq = [p0b.enter_context(nc.psum_tensor(f"p0tq{i}", [128, 512], F32))[:][0:NE, 0:128]
                         for i in range(2)]

                MBg = s_mm.n
                EvBg = s_ev.n

                def g_t(t, MBg=MBg, EvBg=EvBg):
                    s_vv.wait(t)
                    s_gp.wait(t)
                    for blk in range(32):
                        if blk >= 2:
                            t.wait_ge(s_ev.h, EvBg + blk - 1)
                        s_mm.inc(t.transpose(ps_tp[blk % 2],
                                             logits[:, blk * 128:(blk + 1) * 128],
                                             ident_f[0:8, 0:8]))
                block.tensor(g_t)
                assert s_mm.n == MBg + 32, ('s_mm', s_mm.n, MBg + 32)

                def g_s(s, MBg=MBg):
                    for blk in range(32):
                        s.wait_ge(s_mm.h, MBg + blk + 1)
                        s_ev.inc(s.copy(lg_tm[:, blk * 8:(blk + 1) * 8], ps_tp[blk % 2]))
                block.scalar(g_s)
                assert s_ev.n == EvBg + 32, ('s_ev', s_ev.n, EvBg + 32)
                evg = s_ev.n

                def g_v(v):
                    v.wait_ge(s_ev.h, evg)
                    l3 = lg_tm.rearrange("p (g e) -> p g e", e=8)
                    m1 = gs[0][:, 0:32]
                    v.reduce_max(m1.rearrange("p (g o) -> p g o", o=1), l3, axis=AX.X)
                    d = gs[1]
                    v.tensor_tensor(d.rearrange("p (g e) -> p g e", e=8), l3,
                                    grp0(m1, 8), ALU.subtract)
                    z = gs[2]
                    v.tensor_single_scalar(z, d, 0.0, ALU.is_equal)
                    v.scalar_tensor_tensor(z, z, -1e30, d, ALU.mult, ALU.add)
                    m2 = gs[0][:, 32:64]
                    v.reduce_max(m2.rearrange("p (g o) -> p g o", o=1),
                                 z.rearrange("p (g e) -> p g e", e=8), axis=AX.X)
                    keep = gs[2]
                    s_vv.inc(v.tensor_tensor(keep.rearrange("p (g e) -> p g e", e=8),
                                             d.rearrange("p (g e) -> p g e", e=8),
                                             grp0(m2, 8), ALU.is_ge))
                block.vector(g_v)
                vvg = s_vv.n

                def g_v1b(v):
                    v.wait_ge(s_vv.h, vvg)
                    m2 = gs[0][:, 32:64]
                    s_vv.inc(v.scalar_tensor_tensor(
                        gs[1].rearrange("p (g e) -> p g e", e=8),
                        gs[1].rearrange("p (g e) -> p g e", e=8),
                        2.0, grp0(m2, 8), ALU.mult, ALU.subtract))
                block.vector(g_v1b)
                vvg1b = s_vv.n

                def g_s2(s):
                    s.wait_ge(s_vv.h, vvg1b)
                    s_ev.inc(s.activation(gs[1], gs[1], AF.Sigmoid))
                block.scalar(g_s2)
                evg2 = s_ev.n

                def g_v2(v):
                    v.wait_ge(s_ev.h, evg2)
                    s_vv.inc(v.tensor_tensor(gt_tm, gs[1], gs[2], ALU.mult))
                block.vector(g_v2)
                vvg2 = s_vv.n

                MBg2 = s_mm.n
                EvBg2 = s_ev.n

                def g_t2(t, MBg2=MBg2, EvBg2=EvBg2):
                    t.wait_ge(s_vv.h, vvg2)
                    for blk in range(32):
                        if blk >= 2:
                            t.wait_ge(s_ev.h, EvBg2 + blk - 1)
                        s_mm.inc(t.transpose(ps_tq[blk % 2],
                                             gt_tm[:, blk * 8:(blk + 1) * 8], ident_f))
                block.tensor(g_t2)
                assert s_mm.n == MBg2 + 32, ('s_mm', s_mm.n, MBg2 + 32)

                def g_s3(s, MBg2=MBg2):
                    for blk in range(32):
                        s.wait_ge(s_mm.h, MBg2 + blk + 1)
                        s_ev.inc(s.copy(gates_bf[:, blk * 128:(blk + 1) * 128],
                                        ps_tq[blk % 2]))
                block.scalar(g_s3)
                assert s_ev.n == EvBg2 + 32, ('s_ev', s_ev.n, EvBg2 + 32)

                def g_spill(sync, EvBg2=EvBg2):
                    sync.wait_ge(s_ev.h, EvBg2 + 32)
                    s_st.dinc(sync.dma_start(out=gatesv_d, in_=gates_bf))
                block.sync(g_spill)

                if debug:
                    def g_dbg(sync):
                        s_ev.wait(sync)
                        s_vv.wait(sync)
                        s_st.dinc(sync.dma_start(out=gates_d, in_=gt_tm))
                        s_st.dinc(sync.dma_start(out=lg_dbg, in_=lg_tm))
                        s_st.dinc(sync.dma_start(out=gs_dbg0, in_=gs[0]))
                        s_st.dinc(sync.dma_start(out=gs_dbg1, in_=gs[1]))
                        s_st.dinc(sync.dma_start(out=gs_dbg2, in_=gs[2]))
                    block.sync(g_dbg)

            barrier()

        with contextlib.ExitStack() as pw:
            wsb = lambda n_, sh, dt: pw.enter_context(nc.sbuf_tensor(n_, sh, dt))[:]
            NS = 3
            stg = [wsb(f"wstg{i}", [128, 8192], F32) for i in range(NS)]
            stb = [wsb(f"wstb{i}", [128, 8192], BF16) for i in range(NS)]

            jobs = []
            def addjob(src, dst, rows, cols, maxchunks=8):
                nch = rows // 128 if rows >= 128 else 1
                pr = 128 if rows >= 128 else rows
                c0 = 0
                while c0 < max(nch, 1):
                    take = min(maxchunks, nch - c0)
                    if rows >= 128:
                        jobs.append((
                            src.rearrange("(c p) e -> p c e", p=128)[:, c0:c0 + take],
                            dst.rearrange("(c p) e -> p c e", p=128)[:, c0:c0 + take],
                            128, take, cols))
                    else:
                        jobs.append((src, dst, rows, 1, cols))
                    c0 += take

            addjob(w_inT, w_inT_b, DM, 2 * DI, maxchunks=2)
            addjob(xprojT, xprojT_b, DI, 96)
            addjob(dtprojT, dtprojT_b, DTR, DI)
            addjob(woutT, woutT_b, DI, DM)
            addjob(ffw1T, ffw1T_b, DM, DI, maxchunks=4)
            addjob(ffw2T, ffw2T_b, DI, DM)
            for e in range(NE):
                addjob(moe_w1[e], moe_w1_b[e], DM, DE)
                addjob(moe_w2[e], moe_w2_b[e], DE, DM)
            addjob(moe_b2, moe_b2_b, NE, DM)

            ck_wev = [0] * len(jobs)
            ck_wst = [0] * len(jobs)
            for ji, (srcap, dstap, nr, nch, ncl) in enumerate(jobs):
                bw = ji % NS
                tot = nch * ncl

                def w_ld(sync, srcap=srcap, bw=bw, ji=ji, nr=nr, nch=nch, ncl=ncl):
                    if ji >= NS:
                        s_ev.wait(sync, ck_wev[ji - NS])
                    dst = stg[bw][0:nr, 0:nch * ncl]
                    if nch > 1:
                        dst = dst.rearrange("p (c e) -> p c e", c=nch)
                    s_ld.dinc(sync.dma_start(out=dst, in_=srcap))
                block.sync(w_ld)
                ldkw = s_ld.n

                def w_cast(s, bw=bw, ldkw=ldkw, ji=ji, nr=nr, tot=tot):
                    s.wait_ge(s_ld.h, ldkw)
                    if ji >= NS:
                        s_st.wait(s, ck_wst[ji - NS])
                    s_ev.inc(s.copy(stb[bw][0:nr, 0:tot], stg[bw][0:nr, 0:tot]))
                block.scalar(w_cast)
                ck_wev[ji] = s_ev.n
                evkw = s_ev.n

                def w_st(sync, bw=bw, evkw=evkw, dstap=dstap, nr=nr, nch=nch, ncl=ncl):
                    sync.wait_ge(s_ev.h, evkw)
                    src = stb[bw][0:nr, 0:nch * ncl]
                    if nch > 1:
                        src = src.rearrange("p (c e) -> p c e", c=nch)
                    s_st.dinc(sync.dma_start(out=dstap, in_=src))
                block.sync(w_st)
                ck_wst[ji] = s_st.n

            barrier()

        LT1 = 512
        NT1 = L // LT1
        with contextlib.ExitStack() as p1:
            psb = lambda n_, sh, dt: p1.enter_context(nc.sbuf_tensor(n_, sh, dt))[:]
            w_in_sb = psb("p1win", [128, NCM * 2 * DI], BF16)
            xproj_sb = psb("p1xp", [128, NCI * 96], BF16)
            dtproj_sb = psb("p1dtp", [DTR, DI], BF16)
            diagK = psb("p1diagK", [128, NCI * DC * 128], BF16)
            conv_sb = psb("p1convsb", [128, NCI * DC], F32)
            xn_t = [psb(f"p1xn{i}", [128, NCM * LT1], BF16) for i in range(2)]
            xsraw = [psb(f"p1xsr{i}", [128, NCI * (LT1 + 3)], BF16) for i in range(2)]
            zs_r = [psb(f"p1zs{i}", [128, LT1], BF16) for i in range(4)]
            xc_r = [psb(f"p1xc{i}", [128, LT1], BF16) for i in range(4)]
            pd_r = [psb(f"p1pd{i}", [128, LT1], F32) for i in range(4)]
            dtr_sb = [psb(f"p1dtr{i}", [DTR, LT1], BF16) for i in range(2)]
            pb = [p1.enter_context(nc.psum_tensor(f"p1ps{i}", [128, LT1], F32))[:]
                  for i in range(8)]
            xsr3 = lambda i: xsraw[i].rearrange("p (c l) -> p c l", c=NCI)

            def cv_ld(sync):
                s_ld.dinc(sync.dma_start(out=conv_sb, in_=conv_rel))
            block.sync(cv_ld)
            cvldk = s_ld.n

            def d_v(v):
                v.wait_ge(s_ld.h, cvldk)
                for c in range(NCI):
                    for k in range(DC):
                        ins = v.tensor_scalar(
                            diagK[:, (c * DC + k) * 128:(c * DC + k + 1) * 128],
                            ident_b, conv_sb[:, c * DC + k:c * DC + k + 1],
                            None, ALU.mult)
                for i in range(2):
                    for c in range(NCI):
                        ins = v.memset(xsr3(i)[:, c, 0:3], 0.0)
                s_vv.inc(ins)
            block.vector(d_v)
            vvd = s_vv.n

            def w_ld(sync):
                s_ld.dinc(sync.dma_start(
                    out=w_in_sb.rearrange("p (c e) -> p c e", c=NCM), in_=w_inT_b.rearrange("(c p) e -> p c e", p=128)))
                s_ld.dinc(sync.dma_start(
                    out=xproj_sb.rearrange("p (c e) -> p c e", c=NCI), in_=xprojT_b.rearrange("(c p) e -> p c e", p=128)))
                s_ld.dinc(sync.dma_start(out=dtproj_sb, in_=dtprojT_b))
            block.sync(w_ld)
            wldk = s_ld.n

            bank_ck = [(None, 0)] * 8
            ck_xn_cons = [0] * NT1
            ck_xsraw_cons = [0] * NT1
            ck_xc_prev = {}
            ck_halo = [vvd] * NT1

            def bank_wait(t, bank):
                sem, cnt = bank_ck[bank]
                if sem is not None:
                    t.wait_ge(sem.h, cnt)

            for ti in range(NT1):
                b = ti % 2
                l0 = ti * LT1

                def ld(sync, b=b, l0=l0, ti=ti):
                    if ti >= 2:
                        s_mm.wait(sync, ck_xn_cons[ti - 2])
                    s_ld.dinc(sync.dma_start(
                        out=xn_t[b].rearrange("p (c l) -> p c l", c=NCM),
                        in_=xn_bf_d.rearrange("(c p) l -> p c l", p=128)[:, :, l0:l0 + LT1]))
                block.sync(ld)
                ldk = s_ld.n

                if ti > 0:
                    def halo(v, b=b, ti=ti):
                        s_ev.wait(v)
                        if ti >= 2:
                            v.wait_ge(s_mm.h, ck_xsraw_cons[ti - 2])
                        for c in range(NCI):
                            ins = v.tensor_copy(xsr3(b)[:, c, 0:3],
                                                xsr3(1 - b)[:, c, LT1:LT1 + 3])
                        s_vv.inc(ins)
                    block.vector(halo)
                    ck_halo[ti] = s_vv.n

                MBa = s_mm.n
                EvBa = s_ev.n

                def t_xs(t, b=b, ldk=ldk, ti=ti, EvBa=EvBa):
                    t.wait_ge(s_ld.h, max(ldk, wldk))
                    t.wait_ge(s_vv.h, ck_halo[ti])
                    for m in range(16):
                        bank = m % 7
                        if m >= 7:
                            t.wait_ge(s_ev.h, EvBa + m - 7 + 1)
                        else:
                            sem, cnt = bank_ck[bank]
                            if sem is not None:
                                t.wait_ge(sem.h, cnt)
                        for k in range(NCM):
                            ins = t.matmul(
                                pb[bank],
                                w_in_sb[:, k * 2 * DI + m * 128:k * 2 * DI + (m + 1) * 128],
                                xn_t[b].rearrange("p (c l) -> p c l", c=NCM)[:, k],
                                start=(k == 0), stop=(k == NCM - 1))
                        s_mm.inc(ins)
                block.tensor(t_xs)
                assert s_mm.n == MBa + 16, ('s_mm', s_mm.n, MBa + 16)

                def s_xs(s, b=b, MBa=MBa):
                    for m in range(16):
                        s.wait_ge(s_mm.h, MBa + m + 1)
                        s_ev.inc(s.copy(xsr3(b)[:, m, 3:3 + LT1], pb[m % 7]))
                block.scalar(s_xs)
                assert s_ev.n == EvBa + 16, ('s_ev', s_ev.n, EvBa + 16)
                for m in range(9, 16):
                    bank_ck[m % 7] = (s_ev, EvBa + m + 1)

                MBb = s_mm.n
                EvBb = s_ev.n
                SBb = s_st.n

                def t_cv(t, b=b, EvBb=EvBb, MBb=MBb):
                    sem, cnt = bank_ck[7]
                    if sem is not None:
                        t.wait_ge(sem.h, cnt)
                    for c in range(NCI):
                        bank = c % 7
                        if c >= 7:
                            t.wait_ge(s_ev.h, EvBb + c - 7 + 1)
                        else:
                            sem, cnt = bank_ck[bank]
                            if sem is not None:
                                t.wait_ge(sem.h, cnt)
                        for k in range(DC):
                            ins = t.matmul(
                                pb[bank], diagK[:, (c * DC + k) * 128:(c * DC + k + 1) * 128],
                                xsr3(b)[:, c, k:k + LT1],
                                start=(k == 0), stop=(k == DC - 1))
                        s_mm.inc(ins)
                        t.wait_ge(s_ev.h, EvBb + c + 1)
                        s_mm.inc(t.matmul(pb[7][0:96, :], xproj_sb[:, c * 96:(c + 1) * 96],
                                          xc_r[c % 4], start=(c == 0), stop=(c == NCI - 1)))
                block.tensor(t_cv)
                assert s_mm.n == MBb + 32, ('s_mm', s_mm.n, MBb + 32)

                def s_cv(s, b=b, l0=l0, MBb=MBb, EvBb=EvBb, SBb=SBb):
                    for c in range(NCI):
                        s.wait_ge(s_mm.h, MBb + 2 * c + 1)
                        if c >= 4:
                            s.wait_ge(s_mm.h, MBb + 2 * (c - 4) + 2)
                            s.wait_ge(s_st.h, SBb + 16 * (c - 3))
                        else:
                            key = ("xc", c)
                            if key in ck_xc_prev:
                                s.wait_ge(s_mm.h, ck_xc_prev[key][0])
                                s.wait_ge(s_st.h, ck_xc_prev[key][1])
                        s_ev.inc(s.activation(xc_r[c % 4], pb[c % 7], AF.Silu,
                                              bias=p_convb[:, c:c + 1]))
                    s.wait_ge(s_mm.h, MBb + 32)
                    s.copy(dtr_sb[b], pb[7][0:DTR, :])
                    s_ev.inc(s.copy(bcm_bf[64:96, l0:l0 + LT1], pb[7][64:96, :]))
                block.scalar(s_cv)
                assert s_ev.n == EvBb + 17, ('s_ev', s_ev.n, EvBb + 17)
                for c in range(9, 16):
                    bank_ck[c % 7] = (s_ev, EvBb + c + 1)
                bank_ck[7] = (s_ev, EvBb + 17)
                xp_ev = EvBb + 17
                for c in range(12, 16):
                    ck_xc_prev[("xc", c - 12)] = (MBb + 2 * c + 2, SBb + 16 * (c + 1))

                def st_xc(sync, l0=l0, EvBb=EvBb):
                    for c in range(NCI):
                        sync.wait_ge(s_ev.h, EvBb + c + 1)
                        s_st.dinc(sync.dma_start(
                            out=xsc_d[c * 128:(c + 1) * 128, l0:l0 + LT1], in_=xc_r[c % 4]))
                block.sync(st_xc)
                assert s_st.n == SBb + 16 * 16, ('s_st', s_st.n, SBb + 16 * 16)

                MBc = s_mm.n
                EvBc = s_ev.n
                SBc = s_st.n

                def t_dt(t, b=b, xp_ev=xp_ev, EvBc=EvBc):
                    t.wait_ge(s_ev.h, xp_ev)
                    for m in range(NCI):
                        bank = m % 7
                        if m >= 7:
                            t.wait_ge(s_ev.h, EvBc + m - 7 + 1)
                        else:
                            sem, cnt = bank_ck[bank]
                            if sem is not None:
                                t.wait_ge(sem.h, cnt)
                        s_mm.inc(t.matmul(pb[bank], dtproj_sb[:, m * 128:(m + 1) * 128],
                                          dtr_sb[b], start=True, stop=True))
                block.tensor(t_dt)
                assert s_mm.n == MBc + 16, ('s_mm', s_mm.n, MBc + 16)

                def s_dt(s, MBc=MBc, SBc=SBc, b=b):
                    for m in range(NCI):
                        s.wait_ge(s_mm.h, MBc + m + 1)
                        if m >= 4:
                            s.wait_ge(s_st.h, SBc + 16 * (m - 3))
                        else:
                            key = ("pd", m)
                            if key in ck_xc_prev:
                                s.wait_ge(s_st.h, ck_xc_prev[key])
                        s_ev.inc(s.activation(pd_r[m % 4], pb[m % 7], AF.Identity,
                                              bias=p_dtb[:, m:m + 1]))
                block.scalar(s_dt)
                assert s_ev.n == EvBc + 16, ('s_ev', s_ev.n, EvBc + 16)
                for m in range(9, 16):
                    bank_ck[m % 7] = (s_ev, EvBc + m + 1)
                for m in range(12, 16):
                    ck_xc_prev[("pd", m - 12)] = SBc + 16 * (m + 1)

                def st_pd(sync, l0=l0, EvBc=EvBc):
                    for m in range(NCI):
                        sync.wait_ge(s_ev.h, EvBc + m + 1)
                        s_st.dinc(sync.dma_start(
                            out=predt_d[m * 128:(m + 1) * 128, l0:l0 + LT1], in_=pd_r[m % 4]))
                block.sync(st_pd)
                assert s_st.n == SBc + 16 * 16, ('s_st', s_st.n, SBc + 16 * 16)

                MBd = s_mm.n
                EvBd = s_ev.n
                SBd = s_st.n

                def t_z(t, b=b, EvBd=EvBd):
                    for m in range(16):
                        bank = m % 7
                        if m >= 7:
                            t.wait_ge(s_ev.h, EvBd + m - 7 + 1)
                        else:
                            sem, cnt = bank_ck[bank]
                            if sem is not None:
                                t.wait_ge(sem.h, cnt)
                        for k in range(NCM):
                            ins = t.matmul(
                                pb[bank],
                                w_in_sb[:, k * 2 * DI + (16 + m) * 128:k * 2 * DI + (17 + m) * 128],
                                xn_t[b].rearrange("p (c l) -> p c l", c=NCM)[:, k],
                                start=(k == 0), stop=(k == NCM - 1))
                        s_mm.inc(ins)
                block.tensor(t_z)
                assert s_mm.n == MBd + 16, ('s_mm', s_mm.n, MBd + 16)
                ck_xn_cons[ti] = MBd + 16
                ck_xsraw_cons[ti] = MBb + 32

                def s_z(s, MBd=MBd, SBd=SBd, b=b):
                    for m in range(16):
                        s.wait_ge(s_mm.h, MBd + m + 1)
                        if m >= 4:
                            s.wait_ge(s_st.h, SBd + 16 * (m - 3))
                        else:
                            key = ("zs", m)
                            if key in ck_xc_prev:
                                s.wait_ge(s_st.h, ck_xc_prev[key])
                        s_ev.inc(s.activation(zs_r[m % 4], pb[m % 7], AF.Silu))
                block.scalar(s_z)
                assert s_ev.n == EvBd + 16, ('s_ev', s_ev.n, EvBd + 16)
                for m in range(9, 16):
                    bank_ck[m % 7] = (s_ev, EvBd + m + 1)
                for m in range(12, 16):
                    ck_xc_prev[("zs", m - 12)] = SBd + 16 * (m + 1)

                def st_z(sync, l0=l0, EvBd=EvBd):
                    for m in range(16):
                        sync.wait_ge(s_ev.h, EvBd + m + 1)
                        s_st.dinc(sync.dma_start(
                            out=zsil_d[m * 128:(m + 1) * 128, l0:l0 + LT1], in_=zs_r[m % 4]))
                block.sync(st_z)
                assert s_st.n == SBd + 16 * 16, ('s_st', s_st.n, SBd + 16 * 16)

            barrier()

        LT2 = 1024
        NT2 = L // LT2
        with contextlib.ExitStack() as p2:
            psb = lambda n_, sh, dt: p2.enter_context(nc.sbuf_tensor(n_, sh, dt))[:]
            pd_one = psb("p2pd", [128, LT2], F32)
            pd_t = [pd_one, pd_one]
            xs_t = [psb(f"p2xs{i}", [128, LT2], BF16) for i in range(2)]
            tA = psb("p2tA", [128, LT2], F32)
            dt_one = psb("p2dt", [128, LT2], F32)
            dt_f = [dt_one, dt_one]
            dtxs_one = psb("p2dtxs", [128, LT2], BF16)
            dtxs = [dtxs_one, dtxs_one]
            bb_one = psb("p2bb", [128, DS * LT2], BF16)
            cb_one = psb("p2cb", [128, DS * LT2], BF16)
            bb_t = [bb_one, bb_one]
            cb_t = [cb_one, cb_one]
            da_t = [psb(f"p2da{i}", [128, LT2], F32) for i in range(2)]
            dbx_one = psb("p2dbx", [128, LT2], BF16)
            dbx_t = [dbx_one, dbx_one]
            h_t = [psb(f"p2h{i}", [128, LT2], BF16) for i in range(2)]
            hc_t = [psb(f"p2hc{i}", [128, LT2], BF16) for i in range(2)]
            y_one = psb("p2y", [128, LT2], BF16)
            y_t = [y_one, y_one]
            py = [p2.enter_context(nc.psum_tensor(f"p2py{i}", [128, LT2], F32))[:]
                  for i in range(2)]
            LT5 = 256
            NT5 = L // LT5
            m_w1 = psb("p5w1", [128, 4 * NCM * DE], BF16)
            m_w2 = psb("p5w2", [128, 4 * 4 * DM], BF16)
            m_b2 = psb("p5b2", [NE, DM], BF16)
            m_xn = psb("p5xn", [128, NCM * LT5], BF16)
            m_gam = psb("p5gam", [128, 16 * LT5], BF16)
            m_gb = psb("p5gb", [128, 4 * LT5], BF16)
            m_acc = psb("p5acc", [128, NCM * LT5], BF16)
            g_sl = [psb(f"p5gsl{i}", [NE, LT5], BF16) for i in range(2)]
            p5p1 = [p2.enter_context(
                nc.psum_tensor(f"p5ps1{i}", [128, 512], F32))[:][:, 0:LT5]
                for i in range(2)]
            p5p2 = [p2.enter_context(
                nc.psum_tensor(f"p5ps2{i}", [128, 512], F32))[:][:, 0:LT5]
                for i in range(2)]
            m_acc3 = m_acc.rearrange("p (q l) -> p q l", q=NCM)

            p5ck = {}
            wld1 = [0, 0]
            wld2 = [0, 0]

            def p5_ld(j):
                half5, ti5 = j // 16, j % 16
                es5 = (0, 1, 2, 3) if half5 == 0 else (4, 5, 6, 7)
                b5, l05 = j % 2, ti5 * LT5
                if j == 0:
                    def wl0(sync):
                        s_ld.dinc(sync.dma_start(out=m_b2, in_=moe_b2_b))
                        for i, e in enumerate(es5):
                            s_ld.dinc(sync.dma_start(
                                out=m_w1[:, i * NCM * DE:(i + 1) * NCM * DE].rearrange(
                                    "p (q j) -> p q j", q=NCM),
                                in_=moe_w1_b[e].rearrange("(q p) j -> p q j", p=128)))
                        wld1[0] = s_ld.n
                        for i, e in enumerate(es5):
                            s_ld.dinc(sync.dma_start(
                                out=m_w2[:, i * 4 * DM:(i + 1) * 4 * DM].rearrange(
                                    "p (q d) -> p q d", q=4),
                                in_=moe_w2_b[e].rearrange("(q p) d -> p q d", p=128)))
                        wld2[0] = s_ld.n
                    block.sync(wl0)
                if j == 16:
                    def wl1(sync):
                        s_mm.wait(sync, p5ck[15]["t15_end"])
                        for i, e in enumerate(es5):
                            s_ld.dinc(sync.dma_start(
                                out=m_w1[:, i * NCM * DE:(i + 1) * NCM * DE].rearrange(
                                    "p (q j) -> p q j", q=NCM),
                                in_=moe_w1_b[e].rearrange("(q p) j -> p q j", p=128)))
                        wld1[1] = s_ld.n
                    block.sync(wl1)

                def ld5(sync, j=j, b5=b5, l05=l05, es5=es5):
                    if j >= 1:
                        s_mm.wait(sync, p5ck[j - 1]["t15_end"])
                        s_gp.wait(sync, p5ck[j - 1]["gate"])
                    s_ld.dinc(sync.dma_start(out=g_sl[b5],
                                             in_=gatesv_d[:, l05:l05 + LT5]))
                    gslk = s_ld.n
                    s_ld.dinc(sync.dma_start(
                        out=m_xn.rearrange("p (q l) -> p q l", q=NCM),
                        in_=xn_bf_d.rearrange("(q p) l -> p q l",
                                              p=128)[:, :, l05:l05 + LT5]))
                    sync.wait_ge(s_ld.h, gslk)
                    for i, e in enumerate(es5):
                        s_ld.dinc(sync.dma_start(
                            out=m_gb[:, i * LT5:(i + 1) * LT5],
                            in_=brow(g_sl[b5][e:e + 1, :])))
                block.sync(ld5)
                p5ck[j] = {"ld": s_ld.n}

            ck_vend = [0] * NT2
            prev_dtv = [0]
            prev_hc = [None]
            ck_pd = {}
            ck_xs = {}
            ck_hc = []
            ck_da = {}
            ck_y = {}
            ck_yst = {}

            if debug:
                def bcmdump(sync):
                    s_st.dinc(sync.dma_start(out=bcm_d, in_=bcm_bf))
                block.sync(bcmdump)

            hstdump_ck = [0]
            for lt in range(NT2):
                l0 = lt * LT2
                lb = lt % 2

                def bc(sync, l0=l0, lt=lt, lb=lb):
                    if lt >= 1:
                        s_vv.wait(sync, ck_vend[lt - 1])
                    for n in range(DS):
                        s_ld.dinc(sync.dma_start(
                            out=bb_t[lb][:, n * LT2:(n + 1) * LT2],
                            in_=brow(bcm_bf[64 + n:65 + n, l0:l0 + LT2])))
                        s_ld.dinc(sync.dma_start(
                            out=cb_t[lb][:, n * LT2:(n + 1) * LT2],
                            in_=brow(bcm_bf[80 + n:81 + n, l0:l0 + LT2])))
                block.sync(bc)
                bck = s_ld.n

                for c in range(NCI):
                    cb = c % 2
                    seq = lt * NCI + c

                    def ld(sync, c=c, cb=cb, l0=l0, lt=lt, seq=seq):
                        if seq >= 1:
                            pc, pl = (c - 1, lt) if c >= 1 else (NCI - 1, lt - 1)
                            s_ev.wait(sync, ck_pd[(pl, pc)])
                        if seq >= 2:
                            pc, pl = (c - 2, lt) if c >= 2 else (NCI - 2 + c, lt - 1)
                            s_mm.wait(sync, ck_xs[(pl, pc)][0])
                            s_vv.wait(sync, ck_xs[(pl, pc)][1])
                        s_ld.dinc(sync.dma_start(
                            out=pd_t[cb], in_=predt_d[c * 128:(c + 1) * 128, l0:l0 + LT2]))
                        s_ld.dinc(sync.dma_start(
                            out=xs_t[cb], in_=xsc_d[c * 128:(c + 1) * 128, l0:l0 + LT2]))
                    block.sync(ld)
                    ldk = s_ld.n

                    j15 = seq // 2 - 1 if (seq % 2 == 0 and seq >= 2) else None
                    if j15 is not None and j15 > 30:
                        j15 = None
                    j25 = (seq - 3) // 2 if (seq % 2 == 1 and seq >= 3) else None
                    if j25 is not None and j25 > 30:
                        j25 = None
                    jld = (seq - 1) // 2 if seq % 2 == 1 else None
                    if jld is not None and jld <= 30:
                        p5_ld(jld)
                    if seq == 34:
                        def wl2(sync):
                            s_mm.wait(sync, p5ck[15]["t25_end"])
                            for i, e in enumerate((4, 5, 6, 7)):
                                s_ld.dinc(sync.dma_start(
                                    out=m_w2[:, i * 4 * DM:(i + 1) * 4 * DM].rearrange(
                                        "p (q d) -> p q d", q=4),
                                    in_=moe_w2_b[e].rearrange("(q p) d -> p q d",
                                                              p=128)))
                            wld2[1] = s_ld.n
                        block.sync(wl2)

                    EvB = s_ev.n
                    VB = s_vv.n
                    MB = s_mm.n
                    if j15 is not None:
                        def daev(n, EvB=EvB):
                            return EvB + 2 + n if n < 2 else EvB + 2 * n
                        def gelinc(g, EvB=EvB):
                            return EvB + 2 * g + 5 if g <= 13 else EvB + 19 + g
                        accinc = None
                        evtot = 34
                        hc_abs = [MB + 2 * n + 2 for n in range(DS)]
                    elif j25 is not None:
                        def daev(n, EvB=EvB):
                            return EvB + 2 + n + max(0, n - 2) // 2
                        def accinc(m, EvB=EvB):
                            return EvB + 3 * m + 6 if m <= 6 else EvB + 26
                        gelinc = None
                        evtot = 26
                        hc_abs = [MB + 3 * (n // 2) + 1 if n % 2 == 0
                                  else MB + 3 * ((n - 1) // 2) + 3
                                  for n in range(DS)]
                    else:
                        def daev(n, EvB=EvB):
                            return EvB + 2 + n
                        gelinc = accinc = None
                        evtot = 18
                        hc_abs = [MB + n + 1 for n in range(DS)]

                    def p2s(sc, cb=cb, ldk=ldk, seq=seq, c=c, lt=lt, EvB=EvB,
                            VB=VB, MB=MB, j15=j15, j25=j25):
                        if j15 is not None:
                            es5 = (0, 1, 2, 3) if j15 < 16 else (4, 5, 6, 7)

                            def gel(g):
                                sc.wait_ge(s_mm.h, MB + 2 * g + 1)
                                if g < 2 and j15 >= 1:
                                    sc.wait_ge(s_mm.h, p5ck[j15 - 1]["t25_end"])
                                e5 = es5[g // 4]
                                s_ev.inc(sc.activation(
                                    m_gam[:, g * LT5:(g + 1) * LT5], p5p1[g % 2],
                                    AF.Gelu,
                                    bias=p_moeb1[:, e5 * 4 + g % 4:
                                                 e5 * 4 + g % 4 + 1]))
                        if j25 is not None:
                            def acc(m):
                                sc.wait_ge(s_mm.h, MB + 3 * m + 2)
                                if m == 0 and j25 >= 1:
                                    sc.wait_ge(s_st.h, p5ck[j25 - 1]["st"])
                                s_ev.inc(sc.copy(m_acc3[:, m], p5p2[m % 2]))
                        sc.wait_ge(s_ld.h, ldk)
                        if seq >= 1:
                            sc.wait_ge(s_vv.h, prev_dtv[0])
                        sc.activation(tA, pd_t[cb], AF.Abs)
                        sc.activation(tA, tA, AF.Exp, scale=-1.0)
                        sc.activation(tA, tA, AF.Ln, bias=1.0)
                        s_ev.inc(sc.activation(dt_f[cb], pd_t[cb], AF.Relu))
                        for n in range(DS):
                            nb = n % 2
                            col = c * DS + n
                            if n == 0:
                                sc.wait_ge(s_vv.h, VB + 1)
                            if n >= 2:
                                sc.wait_ge(s_vv.h, VB + n)
                                sc.copy(hstate[:, col - 2:col - 1],
                                        h_t[nb][:, LT2 - 1:LT2])
                            s_ev.inc(sc.activation(da_t[nb], dt_f[cb], AF.Exp,
                                                   scale=A_neg[:, col:col + 1]))
                            if j15 is not None and n >= 2:
                                gel(n - 2)
                            if j25 is not None and n >= 3 and n % 2 == 1:
                                acc((n - 3) // 2)
                        sc.wait_ge(s_vv.h, VB + 2 + DS - 1)
                        sc.copy(hstate[:, c * DS + DS - 2:c * DS + DS - 1],
                                h_t[0][:, LT2 - 1:LT2])
                        s_ev.inc(sc.copy(hstate[:, c * DS + DS - 1:c * DS + DS],
                                         h_t[1][:, LT2 - 1:LT2]))
                        if j15 is not None:
                            gel(14)
                            gel(15)
                        if j25 is not None:
                            acc(7)
                    block.scalar(p2s)
                    assert s_ev.n == EvB + evtot, (s_ev.n, EvB, evtot)
                    ck_pd[(lt, c)] = EvB + 1
                    if j15 is not None:
                        p5ck[j15]["gel14"] = EvB + 33
                        p5ck[j15]["gel15"] = EvB + 34
                    if j25 is not None:
                        p5ck[j25]["acc6"] = EvB + 24
                        p5ck[j25]["acc7"] = EvB + 26

                    def p2v(v, cb=cb, c=c, lt=lt, lb=lb, EvB=EvB, MB=MB,
                            bck=bck, daev=daev, hc_abs=hc_abs,
                            prev_hca=prev_hc[0]):
                        v.wait_ge(s_ev.h, EvB + 1)
                        v.wait_ge(s_ld.h, bck)
                        v.tensor_tensor(dt_f[cb], dt_f[cb], tA, ALU.add)
                        s_vv.inc(v.tensor_tensor(dtxs[cb], dt_f[cb], xs_t[cb],
                                                 ALU.mult))
                        for n in range(DS):
                            nb = n % 2
                            col = c * DS + n
                            if n >= 2:
                                v.wait_ge(s_mm.h, hc_abs[n - 2])
                            elif prev_hca is not None:
                                v.wait_ge(s_mm.h, prev_hca[14 + n])
                            v.tensor_tensor(dbx_t[nb], dtxs[cb],
                                            bb_t[lb][:, n * LT2:(n + 1) * LT2], ALU.mult)
                            v.wait_ge(s_ev.h, daev(n))
                            v.tensor_tensor_scan(h_t[nb], da_t[nb], dbx_t[nb],
                                                 hstate[:, col:col + 1], ALU.mult, ALU.add)
                            s_vv.inc(v.tensor_tensor(hc_t[nb], h_t[nb],
                                                     cb_t[lb][:, n * LT2:(n + 1) * LT2],
                                                     ALU.mult))
                    block.vector(p2v)
                    prev_dtv[0] = VB + 1
                    prev_hc[0] = hc_abs
                    ck_xs[(lt, c)] = (hc_abs[0], VB + 1)

                    if debug:
                        def dtdump(sync, c=c, l0=l0, VB=VB, EvB=EvB):
                            sync.wait_ge(s_vv.h, VB + 2)
                            s_ld.dinc(sync.dma_start(
                                out=dt_d[c * 128:(c + 1) * 128, l0:l0 + LT2],
                                in_=dt_f[c % 2]))
                        block.sync(dtdump)

                    def p2t(t, cb=cb, c=c, lt=lt, VB=VB, MB=MB, j15=j15,
                            j25=j25, gelinc=gelinc, accinc=accinc):
                        yp = py[c % 2]
                        if j25 is not None:
                            half5b = j25 // 16
                            b5b = j25 % 2
                        for n in range(DS):
                            nb = n % 2
                            if j15 is not None:
                                g = n
                                i5, jj5 = g // 4, g % 4
                                if g == 0:
                                    t.wait_ge(s_ld.h, max(p5ck[j15]["ld"],
                                                          wld1[j15 // 16]))
                                if g < 2:
                                    if j15 >= 1:
                                        t.wait_ge(s_ev.h, p5ck[j15 - 1][
                                            "gel14" if g == 0 else "gel15"])
                                else:
                                    t.wait_ge(s_ev.h, gelinc(g - 2))
                                for kk in range(NCM):
                                    ins = t.matmul(
                                        p5p1[g % 2],
                                        m_w1[:, (i5 * NCM + kk) * DE + jj5 * 128:
                                             (i5 * NCM + kk) * DE + (jj5 + 1) * 128],
                                        m_xn.rearrange("p (q l) -> p q l",
                                                       q=NCM)[:, kk],
                                        start=(kk == 0), stop=(kk == NCM - 1))
                                s_mm.inc(ins)
                            if j25 is not None and n % 2 == 1:
                                m = (n - 1) // 2
                                if m == 0:
                                    t.wait_ge(s_gp.h, p5ck[j25]["gate"])
                                    t.wait_ge(s_ld.h, wld2[half5b])
                                if m < 2:
                                    if j25 >= 1:
                                        t.wait_ge(s_ev.h, p5ck[j25 - 1][
                                            "acc6" if m == 0 else "acc7"])
                                else:
                                    t.wait_ge(s_ev.h, accinc(m - 2))
                                for q in range(16):
                                    ins = t.matmul(
                                        p5p2[m % 2],
                                        m_w2[:, q * DM + m * 128:q * DM + (m + 1) * 128],
                                        m_gam[:, q * LT5:(q + 1) * LT5],
                                        start=(q == 0),
                                        stop=(half5b == 0 and q == 15))
                                if half5b == 1:
                                    ins = t.matmul(p5p2[m % 2],
                                                   m_b2[:, m * 128:(m + 1) * 128],
                                                   g_sl[b5b], start=False, stop=True)
                                s_mm.inc(ins)
                            t.wait_ge(s_vv.h, VB + 2 + n)
                            if n == 0:
                                pk = (lt, c - 2) if c >= 2 else (lt - 1, NCI - 2 + c)
                                if pk in ck_y:
                                    t.wait_ge(s_vv.h, ck_y[pk])
                                for hh in range(2):
                                    t.matmul(yp[:, hh * 512:(hh + 1) * 512],
                                             diagD[:, c * 128:(c + 1) * 128],
                                             xs_t[cb][:, hh * 512:(hh + 1) * 512],
                                             start=True, stop=False)
                            for hh in range(2):
                                ins = t.matmul(yp[:, hh * 512:(hh + 1) * 512], ident_b,
                                               hc_t[nb][:, hh * 512:(hh + 1) * 512],
                                               start=False, stop=(n == DS - 1))
                            s_mm.inc(ins)
                    block.tensor(p2t)
                    assert s_mm.n == hc_abs[DS - 1], (s_mm.n, MB, hc_abs[DS - 1])
                    if j15 is not None:
                        p5ck[j15]["t15_end"] = MB + 31
                    if j25 is not None:
                        p5ck[j25]["t25_end"] = MB + 23
                    ymmk = hc_abs[DS - 1]

                    def yev(v, c=c, ymmk=ymmk, lt=lt):
                        v.wait_ge(s_mm.h, ymmk)
                        pk = (lt, c - 1) if c >= 1 else (lt - 1, NCI - 1)
                        if pk in ck_yst:
                            v.wait_ge(s_st.h, ck_yst[pk])
                        s_vv.inc(v.tensor_copy(y_t[c % 2], py[c % 2]))
                    block.vector(yev)
                    ck_y[(lt, c)] = s_vv.n
                    yevk = s_vv.n

                    def yst(sync, c=c, yevk=yevk, l0=l0):
                        sync.wait_ge(s_vv.h, yevk)
                        s_st.dinc(sync.dma_start(
                            out=y_d[c * 128:(c + 1) * 128, l0:l0 + LT2], in_=y_t[c % 2]))
                    block.sync(yst)
                    ck_yst[(lt, c)] = s_st.n

                    if j15 is not None:
                        def gate5(g_, j15=j15, gelinc=gelinc):
                            g_.wait_ge(s_ld.h, p5ck[j15]["ld"])
                            for i5 in range(4):
                                g_.wait_ge(s_ev.h, gelinc(4 * i5 + 3))
                                sl = m_gam[:, i5 * 4 * LT5:(i5 + 1) * 4 * LT5
                                           ].rearrange("p (q l) -> p q l", q=4)
                                ins = g_.tensor_tensor(
                                    sl, sl,
                                    rep0(m_gb[:, i5 * LT5:(i5 + 1) * LT5], 4),
                                    ALU.mult)
                            s_gp.inc(ins)
                        block.gpsimd(gate5)
                        p5ck[j15]["gate"] = s_gp.n

                    if j25 is not None:
                        def st5(sync, j25=j25, EvB=EvB):
                            sync.wait_ge(s_ev.h, EvB + 26)
                            dst = macc_d if j25 < 16 else macc2_d
                            l05 = (j25 % 16) * LT5
                            s_st.dinc(sync.dma_start(
                                out=dst.rearrange("(q p) l -> p q l",
                                                  p=128)[:, :, l05:l05 + LT5],
                                in_=m_acc3))
                        block.sync(st5)
                        p5ck[j25]["st"] = s_st.n

                ck_vend[lt] = s_vv.n

            esE = (4, 5, 6, 7)
            b5E, l05E = 1, 15 * LT5

            def ld5e(sync):
                s_mm.wait(sync, p5ck[30]["t15_end"])
                s_gp.wait(sync, p5ck[30]["gate"])
                s_ld.dinc(sync.dma_start(out=g_sl[b5E],
                                         in_=gatesv_d[:, l05E:l05E + LT5]))
                gslk = s_ld.n
                s_ld.dinc(sync.dma_start(
                    out=m_xn.rearrange("p (q l) -> p q l", q=NCM),
                    in_=xn_bf_d.rearrange("(q p) l -> p q l",
                                          p=128)[:, :, l05E:l05E + LT5]))
                sync.wait_ge(s_ld.h, gslk)
                for i, e in enumerate(esE):
                    s_ld.dinc(sync.dma_start(
                        out=m_gb[:, i * LT5:(i + 1) * LT5],
                        in_=brow(g_sl[b5E][e:e + 1, :])))
            block.sync(ld5e)
            ldE = s_ld.n
            MBe = s_mm.n
            EvE = s_ev.n

            def t15e(t):
                t.wait_ge(s_ld.h, max(ldE, wld1[1]))
                for g in range(16):
                    i5, jj5 = g // 4, g % 4
                    if g < 2:
                        t.wait_ge(s_ev.h, p5ck[30]["gel14" if g == 0 else "gel15"])
                    else:
                        t.wait_ge(s_ev.h, EvE + g - 1)
                    for kk in range(NCM):
                        ins = t.matmul(
                            p5p1[g % 2],
                            m_w1[:, (i5 * NCM + kk) * DE + jj5 * 128:
                                 (i5 * NCM + kk) * DE + (jj5 + 1) * 128],
                            m_xn.rearrange("p (q l) -> p q l", q=NCM)[:, kk],
                            start=(kk == 0), stop=(kk == NCM - 1))
                    s_mm.inc(ins)
            block.tensor(t15e)
            assert s_mm.n == MBe + 16

            def s15e(sc):
                for g in range(16):
                    sc.wait_ge(s_mm.h, MBe + g + 1)
                    if g < 2:
                        sc.wait_ge(s_mm.h, p5ck[30]["t25_end"])
                    e5 = esE[g // 4]
                    s_ev.inc(sc.activation(
                        m_gam[:, g * LT5:(g + 1) * LT5], p5p1[g % 2], AF.Gelu,
                        bias=p_moeb1[:, e5 * 4 + g % 4:e5 * 4 + g % 4 + 1]))
            block.scalar(s15e)
            assert s_ev.n == EvE + 16

            def gate5e(g_):
                g_.wait_ge(s_ld.h, ldE)
                for i5 in range(4):
                    g_.wait_ge(s_ev.h, EvE + 4 * i5 + 4)
                    sl = m_gam[:, i5 * 4 * LT5:(i5 + 1) * 4 * LT5].rearrange(
                        "p (q l) -> p q l", q=4)
                    ins = g_.tensor_tensor(
                        sl, sl, rep0(m_gb[:, i5 * LT5:(i5 + 1) * LT5], 4),
                        ALU.mult)
                s_gp.inc(ins)
            block.gpsimd(gate5e)
            gateE = s_gp.n
            MBe2 = s_mm.n
            EvE2 = s_ev.n

            def t25e(t):
                t.wait_ge(s_gp.h, gateE)
                t.wait_ge(s_ld.h, wld2[1])
                for m in range(NCM):
                    if m < 2:
                        t.wait_ge(s_ev.h, p5ck[30]["acc6" if m == 0 else "acc7"])
                    else:
                        t.wait_ge(s_ev.h, EvE2 + m - 1)
                    for q in range(16):
                        t.matmul(p5p2[m % 2],
                                 m_w2[:, q * DM + m * 128:q * DM + (m + 1) * 128],
                                 m_gam[:, q * LT5:(q + 1) * LT5],
                                 start=(q == 0), stop=False)
                    ins = t.matmul(p5p2[m % 2], m_b2[:, m * 128:(m + 1) * 128],
                                   g_sl[b5E], start=False, stop=True)
                    s_mm.inc(ins)
            block.tensor(t25e)
            assert s_mm.n == MBe2 + NCM

            def s25e(sc):
                for m in range(NCM):
                    sc.wait_ge(s_mm.h, MBe2 + m + 1)
                    if m == 0:
                        sc.wait_ge(s_st.h, p5ck[30]["st"])
                    s_ev.inc(sc.copy(m_acc3[:, m], p5p2[m % 2]))
            block.scalar(s25e)
            assert s_ev.n == EvE2 + NCM

            def st5e(sync):
                sync.wait_ge(s_ev.h, EvE2 + NCM)
                s_st.dinc(sync.dma_start(
                    out=macc2_d.rearrange("(q p) l -> p q l",
                                          p=128)[:, :, l05E:l05E + LT5],
                    in_=m_acc3))
            block.sync(st5e)

            barrier()

        LT3 = 512
        NT3 = L // LT3
        with contextlib.ExitStack() as p3:
            psb = lambda n_, sh, dt: p3.enter_context(nc.sbuf_tensor(n_, sh, dt))[:]
            wout_sb = psb("p3wo", [128, NCI * DM], BF16)
            y_sb = [psb(f"p3y{i}", [128, NCI * LT3], BF16) for i in range(2)]
            zs_sb = [psb(f"p3zs{i}", [128, NCI * LT3], BF16) for i in range(2)]
            xn_sb = [psb(f"p3xn{i}", [128, NCM * LT3], BF16) for i in range(2)]
            t_one = psb("p3t", [128, NCM * LT3], F32)
            t_f = [t_one, t_one]
            t_b = psb("p3tb", [128, NCM * LT3], BF16)
            sq_b = psb("p3sq", [128, NCM * LT3], BF16)
            hb_t = [psb(f"p3hb{i}", [128, NCM * LT3], BF16) for i in range(2)]
            mrow = psb("p3m", [1, LT3], F32)
            rrow = psb("p3r", [1, LT3], F32)
            srow = psb("p3s", [1, 4 * LT3], F32)
            m_b = psb("p3mb", [128, LT3], F32)
            r_b = psb("p3rb", [128, LT3], F32)
            pb = [p3.enter_context(nc.psum_tensor(f"p3ps{i}", [128, LT3], F32))[:]
                  for i in range(4)]
            ps_sum = p3.enter_context(nc.psum_tensor("p3psum", [1, LT3], F32))[:]
            ps_sq = p3.enter_context(nc.psum_tensor("p3psq", [1, LT3], F32))[:]

            def w_ld(sync):
                s_ld.dinc(sync.dma_start(
                    out=wout_sb.rearrange("p (c e) -> p c e", c=NCI), in_=woutT_b.rearrange("(c p) e -> p c e", p=128)))
            block.sync(w_ld)
            wldk = s_ld.n

            bank_ck = [0] * 4
            ck_in = [0] * NT3
            ck_stat = [0] * NT3
            ck_hst = [0] * NT3
            ck_hst_dma = [0] * NT3
            for ti in range(NT3):
                b = ti % 2
                l0 = ti * LT3

                def ld(sync, b=b, l0=l0, ti=ti):
                    if ti >= 2:
                        s_vv.wait(sync, ck_in[ti - 2])
                    s_ld.dinc(sync.dma_start(
                        out=y_sb[b].rearrange("p (c l) -> p c l", c=NCI),
                        in_=y_d.rearrange("(c p) l -> p c l", p=128)[:, :, l0:l0 + LT3]))
                    s_ld.dinc(sync.dma_start(
                        out=zs_sb[b].rearrange("p (c l) -> p c l", c=NCI),
                        in_=zsil_d.rearrange("(c p) l -> p c l", p=128)[:, :, l0:l0 + LT3]))
                    s_ld.dinc(sync.dma_start(
                        out=xn_sb[b].rearrange("p (c l) -> p c l", c=NCM),
                        in_=xn_bf_d.rearrange("(c p) l -> p c l", p=128)[:, :, l0:l0 + LT3]))
                block.sync(ld)
                ldk = s_ld.n

                def ysz(v, b=b, ldk=ldk):
                    v.wait_ge(s_ld.h, ldk)
                    s_vv.inc(v.tensor_tensor(y_sb[b], y_sb[b], zs_sb[b], ALU.mult))
                block.vector(ysz)
                yszk = s_vv.n

                MBo = s_mm.n
                VBo = s_vv.n

                def t_o(t, b=b, yszk=yszk, VBo=VBo):
                    t.wait_ge(s_vv.h, yszk)
                    t.wait_ge(s_ld.h, wldk)
                    for m in range(NCM):
                        bank = m % 4
                        if m >= 4:
                            t.wait_ge(s_vv.h, VBo + m - 4 + 1)
                        elif bank_ck[bank]:
                            t.wait_ge(s_vv.h, bank_ck[bank])
                        for k in range(NCI):
                            ins = t.matmul(
                                pb[bank], wout_sb[:, k * DM + m * 128:k * DM + (m + 1) * 128],
                                y_sb[b].rearrange("p (c l) -> p c l", c=NCI)[:, k],
                                start=(k == 0), stop=(k == NCI - 1))
                        s_mm.inc(ins)
                block.tensor(t_o)
                assert s_mm.n == MBo + NCM, ('s_mm', s_mm.n, MBo + NCM)

                def v_o(v, b=b, MBo=MBo, ti=ti):
                    if ti >= 1:
                        v.wait_ge(s_ev.h, ck_hst[ti - 1])
                    for m in range(NCM):
                        v.wait_ge(s_mm.h, MBo + m + 1)
                        s_vv.inc(v.tensor_tensor(
                            t_f[b].rearrange("p (c l) -> p c l", c=NCM)[:, m], pb[m % 4],
                            xn_sb[b].rearrange("p (c l) -> p c l", c=NCM)[:, m], ALU.add))
                block.vector(v_o)
                assert s_vv.n == VBo + NCM, ('s_vv', s_vv.n, VBo + NCM)
                for m in range(4, 8):
                    bank_ck[m % 4] = VBo + m + 1
                ck_in[ti] = s_vv.n
                resk = s_vv.n

                def cast(s, b=b, resk=resk, ti=ti):
                    s.wait_ge(s_vv.h, resk)
                    if ti >= 1:
                        s.wait_ge(s_mm.h, ck_stat[ti - 1])
                    s.copy(t_b, t_f[b])
                    s_ev.inc(s.activation(sq_b, t_f[b], AF.Square))
                block.scalar(cast)
                cak = s_ev.n

                def stat_mm(t, cak=cak):
                    t.wait_ge(s_ev.h, cak)
                    for c in range(NCM):
                        t.matmul(ps_sum, ones_b,
                                 t_b.rearrange("p (c l) -> p c l", c=NCM)[:, c],
                                 start=(c == 0), stop=(c == NCM - 1))
                    for c in range(NCM):
                        ins = t.matmul(ps_sq, ones_b,
                                       sq_b.rearrange("p (c l) -> p c l", c=NCM)[:, c],
                                       start=(c == 0), stop=(c == NCM - 1))
                    s_mm.inc(ins)
                block.tensor(stat_mm)
                ck_stat[ti] = s_mm.n
                stk = s_mm.n

                def stat_s(s, stk=stk):
                    s.wait_ge(s_mm.h, stk)
                    s.mul(mrow, ps_sum, 1.0 / DM)
                    s.activation(srow[0:1, 0 * LT0:1 * LT0], mrow, AF.Square)
                    s_ev.inc(s.mul(srow[0:1, 1 * LT0:2 * LT0], ps_sq, 1.0 / DM))
                block.scalar(stat_s)
                ssk = s_ev.n

                def stat_v(v, ssk=ssk):
                    v.wait_ge(s_ev.h, ssk)
                    v.tensor_tensor(srow[0:1, 2 * LT0:3 * LT0], srow[0:1, 1 * LT0:2 * LT0], srow[0:1, 0 * LT0:1 * LT0], ALU.subtract)
                    s_vv.inc(v.tensor_single_scalar(srow[0:1, 2 * LT0:3 * LT0], srow[0:1, 2 * LT0:3 * LT0],
                                                    1e-12, ALU.add))
                block.vector(stat_v)
                svk = s_vv.n

                def stat_s2(s, svk=svk):
                    s.wait_ge(s_vv.h, svk)
                    s_ev.inc(s.activation(srow[0:1, 3 * LT0:4 * LT0], srow[0:1, 2 * LT0:3 * LT0], AF.Sqrt))
                block.scalar(stat_s2)
                ss2k = s_ev.n

                def stat_v2(v, ss2k=ss2k):
                    v.wait_ge(s_ev.h, ss2k)
                    s_vv.inc(v.reciprocal(rrow, srow[0:1, 3 * LT0:4 * LT0]))
                block.vector(stat_v2)
                sv2k = s_vv.n

                def bcst(sync, sv2k=sv2k):
                    sync.wait_ge(s_vv.h, sv2k)
                    s_ld.dinc(sync.dma_start(out=m_b, in_=brow(mrow)))
                    s_ld.dinc(sync.dma_start(out=r_b, in_=brow(rrow)))
                block.sync(bcst)
                bck = s_ld.n

                def apv(v, b=b, bck=bck):
                    v.wait_ge(s_ld.h, bck)
                    t3 = t_f[b].rearrange("p (c l) -> p c l", c=NCM)
                    v.tensor_tensor(t3, t3, rep0(m_b, NCM), ALU.subtract)
                    s_vv.inc(v.tensor_tensor(t3, t3, rep0(r_b, NCM), ALU.mult))
                block.vector(apv)
                apk = s_vv.n

                def aff(s, b=b, apk=apk, ti=ti):
                    s.wait_ge(s_vv.h, apk)
                    if ti >= 2:
                        s.wait_ge(s_st.h, ck_hst_dma[ti - 2])
                    hb = hb_t[ti % 2]
                    for c in range(NCM):
                        ins = s.activation(hb.rearrange("p (c l) -> p c l", c=NCM)[:, c],
                                           t_f[b].rearrange("p (c l) -> p c l", c=NCM)[:, c],
                                           AF.Identity, bias=p_ln1b[:, c:c + 1],
                                           scale=p_ln1w[:, c:c + 1])
                    s_ev.inc(ins)
                block.scalar(aff)
                ck_hst[ti] = s_ev.n
                afk = s_ev.n

                def hst(sync, ti=ti, afk=afk, l0=l0):
                    sync.wait_ge(s_ev.h, afk)
                    s_st.dinc(sync.dma_start(
                        out=h_d.rearrange("(c p) l -> p c l", p=128)[:, :, l0:l0 + LT3],
                        in_=hb_t[ti % 2].rearrange("p (c l) -> p c l", c=NCM)))
                block.sync(hst)
                ck_hst_dma[ti] = s_st.n

            barrier()

        LT4 = 512
        NT4 = L // LT4
        with contextlib.ExitStack() as p4:
            psb = lambda n_, sh, dt: p4.enter_context(nc.sbuf_tensor(n_, sh, dt))[:]
            w1_sb = psb("p4w1", [128, NCM * DI], BF16)
            w2_sb = psb("p4w2", [128, NCI * DM], BF16)
            h_sb = [psb(f"p4h{i}", [128, NCM * LT4], BF16) for i in range(2)]
            f_sb = [psb(f"p4f{i}", [128, NCI * LT4], BF16) for i in range(2)]
            t4_one = psb("p4t", [128, NCM * LT4], F32)
            t_f = [t4_one, t4_one]
            t_b = psb("p4tb", [128, NCM * LT4], BF16)
            sq_b = psb("p4sq", [128, NCM * LT4], BF16)
            o4_one = psb("p4o", [128, NCM * LT4], BF16)
            o_f = [o4_one, o4_one]
            mrow = psb("p4m", [1, LT4], F32)
            rrow = psb("p4r", [1, LT4], F32)
            srow = psb("p4s", [1, 4 * LT4], F32)
            m_b = psb("p4mb", [128, LT4], F32)
            r_b = psb("p4rb", [128, LT4], F32)
            pb = [p4.enter_context(nc.psum_tensor(f"p4ps{i}", [128, LT4], F32))[:]
                  for i in range(6)]
            ps_sum = p4.enter_context(nc.psum_tensor("p4psum", [1, LT4], F32))[:]
            ps_sq = p4.enter_context(nc.psum_tensor("p4psq", [1, LT4], F32))[:]

            def w_ld(sync):
                s_ld.dinc(sync.dma_start(
                    out=w1_sb.rearrange("p (c e) -> p c e", c=NCM), in_=ffw1T_b.rearrange("(c p) e -> p c e", p=128)))
                s_ld.dinc(sync.dma_start(
                    out=w2_sb.rearrange("p (c e) -> p c e", c=NCI), in_=ffw2T_b.rearrange("(c p) e -> p c e", p=128)))
            block.sync(w_ld)
            wldk = s_ld.n

            bank_ck = [(None, 0)] * 6
            ck_h_cons = [0] * NT4
            ck_f_cons = [0] * NT4
            ck_stat = [0] * NT4
            ck_ost = [0] * NT4
            ck_aff = [0] * NT4

            def bwait(t, bank):
                sem, cnt = bank_ck[bank]
                if sem is not None:
                    t.wait_ge(sem.h, cnt)

            for ti in range(NT4):
                b = ti % 2
                l0 = ti * LT4

                def ld(sync, b=b, l0=l0, ti=ti):
                    if ti >= 2:
                        s_vv.wait(sync, ck_h_cons[ti - 2])
                    s_ld.dinc(sync.dma_start(
                        out=h_sb[b].rearrange("p (c l) -> p c l", c=NCM),
                        in_=h_d.rearrange("(c p) l -> p c l", p=128)[:, :, l0:l0 + LT4]))
                block.sync(ld)
                ldk = s_ld.n

                MB1 = s_mm.n
                EvB1 = s_ev.n

                def t1(t, b=b, ldk=ldk, ti=ti, EvB1=EvB1):
                    t.wait_ge(s_ld.h, max(ldk, wldk))
                    if ti >= 2:
                        t.wait_ge(s_mm.h, ck_f_cons[ti - 2])
                    for m in range(NCI):
                        bank = m % 6
                        if m >= 6:
                            t.wait_ge(s_ev.h, EvB1 + m - 6 + 1)
                        else:
                            sem, cnt = bank_ck[bank]
                            if sem is not None:
                                t.wait_ge(sem.h, cnt)
                        for k in range(NCM):
                            ins = t.matmul(
                                pb[bank], w1_sb[:, k * DI + m * 128:k * DI + (m + 1) * 128],
                                h_sb[b].rearrange("p (c l) -> p c l", c=NCM)[:, k],
                                start=(k == 0), stop=(k == NCM - 1))
                        s_mm.inc(ins)
                block.tensor(t1)
                assert s_mm.n == MB1 + NCI, ('s_mm', s_mm.n, MB1 + NCI)

                def s1(s, b=b, MB1=MB1):
                    for m in range(NCI):
                        s.wait_ge(s_mm.h, MB1 + m + 1)
                        s_ev.inc(s.activation(
                            f_sb[b].rearrange("p (c l) -> p c l", c=NCI)[:, m],
                            pb[m % 6], AF.Prelu, bias=p_ffb1[:, m:m + 1], alpha=0.01))
                block.scalar(s1)
                assert s_ev.n == EvB1 + NCI, ('s_ev', s_ev.n, EvB1 + NCI)
                for m in range(10, 16):
                    bank_ck[m % 6] = (s_ev, EvB1 + m + 1)
                fk = s_ev.n

                MB2 = s_mm.n
                VB2 = s_vv.n

                def t2(t, b=b, fk=fk, VB2=VB2):
                    t.wait_ge(s_ev.h, fk)
                    for m in range(NCM):
                        bank = m % 6
                        if m >= 6:
                            t.wait_ge(s_vv.h, VB2 + m - 6 + 1)
                        else:
                            sem, cnt = bank_ck[bank]
                            if sem is not None:
                                t.wait_ge(sem.h, cnt)
                        for k in range(NCI):
                            ins = t.matmul(
                                pb[bank], w2_sb[:, k * DM + m * 128:k * DM + (m + 1) * 128],
                                f_sb[b].rearrange("p (c l) -> p c l", c=NCI)[:, k],
                                start=(k == 0), stop=(k == NCI - 1))
                        s_mm.inc(ins)
                block.tensor(t2)
                assert s_mm.n == MB2 + NCM, ('s_mm', s_mm.n, MB2 + NCM)
                ck_f_cons[ti] = s_mm.n

                def v2(v, b=b, MB2=MB2, ti=ti):
                    if ti >= 1:
                        v.wait_ge(s_ev.h, ck_aff[ti - 1])
                    for m in range(NCM):
                        v.wait_ge(s_mm.h, MB2 + m + 1)
                        s_vv.inc(v.tensor_tensor(
                            t_f[b].rearrange("p (c l) -> p c l", c=NCM)[:, m], pb[m % 6],
                            h_sb[b].rearrange("p (c l) -> p c l", c=NCM)[:, m], ALU.add))
                block.vector(v2)
                assert s_vv.n == VB2 + NCM, ('s_vv', s_vv.n, VB2 + NCM)
                for m in range(2, 8):
                    bank_ck[m % 6] = (s_vv, VB2 + m + 1)
                ck_h_cons[ti] = s_vv.n
                resk = s_vv.n

                def cast(s, b=b, resk=resk, ti=ti):
                    s.wait_ge(s_vv.h, resk)
                    if ti >= 1:
                        s.wait_ge(s_mm.h, ck_stat[ti - 1])
                    for c in range(NCM):
                        t3 = t_f[b].rearrange("p (c l) -> p c l", c=NCM)[:, c]
                        s.activation(t_b.rearrange("p (c l) -> p c l", c=NCM)[:, c],
                                     t3, AF.Identity, bias=p_ffb2[:, c:c + 1])
                        ins = s.activation(sq_b.rearrange("p (c l) -> p c l", c=NCM)[:, c],
                                           t3, AF.Square, bias=p_ffb2[:, c:c + 1])
                    s_ev.inc(ins)
                block.scalar(cast)
                cak = s_ev.n

                def stat_mm(t, cak=cak):
                    t.wait_ge(s_ev.h, cak)
                    for c in range(NCM):
                        t.matmul(ps_sum, ones_b,
                                 t_b.rearrange("p (c l) -> p c l", c=NCM)[:, c],
                                 start=(c == 0), stop=(c == NCM - 1))
                    for c in range(NCM):
                        ins = t.matmul(ps_sq, ones_b,
                                       sq_b.rearrange("p (c l) -> p c l", c=NCM)[:, c],
                                       start=(c == 0), stop=(c == NCM - 1))
                    s_mm.inc(ins)
                block.tensor(stat_mm)
                ck_stat[ti] = s_mm.n
                stk = s_mm.n

                def stat_s(s, stk=stk):
                    s.wait_ge(s_mm.h, stk)
                    s.mul(mrow, ps_sum, 1.0 / DM)
                    s.activation(srow[0:1, 0 * LT0:1 * LT0], mrow, AF.Square)
                    s_ev.inc(s.mul(srow[0:1, 1 * LT0:2 * LT0], ps_sq, 1.0 / DM))
                block.scalar(stat_s)
                ssk = s_ev.n

                def stat_v(v, ssk=ssk):
                    v.wait_ge(s_ev.h, ssk)
                    v.tensor_tensor(srow[0:1, 2 * LT0:3 * LT0], srow[0:1, 1 * LT0:2 * LT0], srow[0:1, 0 * LT0:1 * LT0], ALU.subtract)
                    s_vv.inc(v.tensor_single_scalar(srow[0:1, 2 * LT0:3 * LT0], srow[0:1, 2 * LT0:3 * LT0],
                                                    1e-12, ALU.add))
                block.vector(stat_v)
                svk = s_vv.n

                def stat_s2(s, svk=svk):
                    s.wait_ge(s_vv.h, svk)
                    s_ev.inc(s.activation(srow[0:1, 3 * LT0:4 * LT0], srow[0:1, 2 * LT0:3 * LT0], AF.Sqrt))
                block.scalar(stat_s2)
                ss2k = s_ev.n

                def stat_v2(v, ss2k=ss2k):
                    v.wait_ge(s_ev.h, ss2k)
                    s_vv.inc(v.reciprocal(rrow, srow[0:1, 3 * LT0:4 * LT0]))
                block.vector(stat_v2)
                sv2k = s_vv.n

                def bcst(sync, sv2k=sv2k):
                    sync.wait_ge(s_vv.h, sv2k)
                    s_ld.dinc(sync.dma_start(out=m_b, in_=brow(mrow)))
                    s_ld.dinc(sync.dma_start(out=r_b, in_=brow(rrow)))
                block.sync(bcst)
                bck = s_ld.n

                def apv(v, b=b, bck=bck):
                    v.wait_ge(s_ld.h, bck)
                    t3 = t_f[b].rearrange("p (c l) -> p c l", c=NCM)
                    for c in range(NCM):
                        v.scalar_tensor_tensor(t3[:, c], t3[:, c], p_ffb2[:, c:c + 1],
                                               m_b, ALU.add, ALU.subtract)
                    s_vv.inc(v.tensor_tensor(t3, t3, rep0(r_b, NCM), ALU.mult))
                block.vector(apv)
                apk = s_vv.n

                def aff(s, b=b, apk=apk, ti=ti):
                    s.wait_ge(s_vv.h, apk)
                    if ti >= 1:
                        s.wait_ge(s_st.h, ck_ost[ti - 1])
                    of = o_f[ti % 2]
                    for c in range(NCM):
                        ins = s.activation(of.rearrange("p (c l) -> p c l", c=NCM)[:, c],
                                           t_f[b].rearrange("p (c l) -> p c l", c=NCM)[:, c],
                                           AF.Identity, bias=p_ln2b[:, c:c + 1],
                                           scale=p_ln2w[:, c:c + 1])
                    s_ev.inc(ins)
                block.scalar(aff)
                afk = s_ev.n
                ck_aff[ti] = afk

                def ost(sync, ti=ti, afk=afk, l0=l0):
                    sync.wait_ge(s_ev.h, afk)
                    s_st.dinc(sync.dma_start(
                        out=attn_d.rearrange("(c p) l -> p c l", p=128)[:, :, l0:l0 + LT4],
                        in_=o_f[ti % 2].rearrange("p (c l) -> p c l", c=NCM)))
                block.sync(ost)
                ck_ost[ti] = s_st.n

            barrier()

        LT6 = 512
        NT6 = L // LT6
        with contextlib.ExitStack() as p6:
            psb = lambda n_, sh, dt: p6.enter_context(nc.sbuf_tensor(n_, sh, dt))[:]
            x_sb = [psb(f"p6x{i}", [128, NCM * LT6], F32) for i in range(2)]
            at_sb = [psb(f"p6at{i}", [128, NCM * LT6], BF16) for i in range(2)]
            mo_sb = [psb(f"p6mo{i}", [128, NCM * LT6], BF16) for i in range(2)]
            mo2_sb = [psb(f"p6mo2{i}", [128, NCM * LT6], BF16) for i in range(2)]
            o_sb = [psb(f"p6o{i}", [128, NCM * LT6], F32) for i in range(2)]

            ck_o = [0] * NT6
            ck_ost6 = [0] * NT6
            for ti in range(NT6):
                b = ti % 2
                l0 = ti * LT6

                def ld(sync, b=b, l0=l0, ti=ti):
                    if ti >= 2:
                        s_vv.wait(sync, ck_o[ti - 2])
                    for dst, src in ((x_sb[b], x_fm), (at_sb[b], attn_d),
                                     (mo_sb[b], macc_d), (mo2_sb[b], macc2_d)):
                        s_ld.dinc(sync.dma_start(
                            out=dst.rearrange("p (c l) -> p c l", c=NCM),
                            in_=src.rearrange("(c p) l -> p c l", p=128)[:, :, l0:l0 + LT6]))
                block.sync(ld)
                ldk = s_ld.n

                def vsum(v, b=b, ldk=ldk, ti=ti):
                    v.wait_ge(s_ld.h, ldk)
                    if ti >= 2:
                        v.wait_ge(s_st.h, ck_ost6[ti - 2])
                    v.tensor_tensor(o_sb[b], x_sb[b], at_sb[b], ALU.add)
                    v.tensor_tensor(x_sb[b], mo_sb[b], mo2_sb[b], ALU.add)
                    s_vv.inc(v.tensor_tensor(o_sb[b], o_sb[b], x_sb[b], ALU.add))
                block.vector(vsum)
                ck_o[ti] = s_vv.n
                vk = s_vv.n

                def ost(sync, b=b, vk=vk, l0=l0):
                    sync.wait_ge(s_vv.h, vk)
                    s_st.dinc(sync.dma_start(
                        out=out_fm.rearrange("(c p) l -> p c l", p=128)[:, :, l0:l0 + LT6],
                        in_=o_sb[b].rearrange("p (c l) -> p c l", c=NCM)))
                block.sync(ost)
                ck_ost6[ti] = s_st.n

            def fin(sync):
                s_st.wait(sync)
                s_ld.wait(sync)
            block.sync(fin)

    return nc


def _relay(a, nchunks, inner):
    a = np.asarray(a)
    return np.ascontiguousarray(
        a.reshape(nchunks, 128, inner).transpose(1, 0, 2).reshape(128, nchunks * inner))


def _moeb1_relay(mb1):
    out = np.zeros((128, NE * 4), np.float32)
    for e in range(NE):
        for j in range(4):
            out[:, e * 4 + j] = mb1[e, j * 128:(j + 1) * 128]
    return out


def make_shared(inputs):
    f = lambda k: np.ascontiguousarray(np.asarray(inputs[k], dtype=np.float32))
    return dict(
        w_inT=np.ascontiguousarray(f("L0_attn_in_proj_w").T),
        conv_rel=_relay(f("L0_attn_conv_w"), NCI, DC),
        convb_rel=_relay(f("L0_attn_conv_b")[:, None], NCI, 1),
        xprojT=np.ascontiguousarray(f("L0_attn_x_proj_w").T),
        dtprojT=np.ascontiguousarray(f("L0_attn_dt_proj_w").T),
        dtb_rel=_relay(f("L0_attn_dt_proj_b")[:, None], NCI, 1),
        A_rel=_relay(f("L0_attn_A_log"), NCI, DS),
        D_rel=_relay(f("L0_attn_D")[:, None], NCI, 1),
        woutT=np.ascontiguousarray(f("L0_attn_out_proj_w").T),
        ln1w_rel=_relay(f("L0_attn_ln1_w")[:, None], NCM, 1),
        ln1b_rel=_relay(f("L0_attn_ln1_b")[:, None], NCM, 1),
        ffw1T=np.ascontiguousarray(f("L0_attn_ffn_w1").T),
        ffb1_rel=_relay(f("L0_attn_ffn_b1")[:, None], NCI, 1),
        ffw2T=np.ascontiguousarray(f("L0_attn_ffn_w2").T),
        ffb2_rel=_relay(f("L0_attn_ffn_b2")[:, None], NCM, 1),
        ln2w_rel=_relay(f("L0_attn_ln2_w")[:, None], NCM, 1),
        ln2b_rel=_relay(f("L0_attn_ln2_b")[:, None], NCM, 1),
        moe_w1=f("L0_ffn_w1"),
        moeb1_rel=_moeb1_relay(f("L0_ffn_b1")),
        moe_w2=f("L0_ffn_w2"),
        moe_b2=f("L0_ffn_b2"),
        centT=np.ascontiguousarray(f("L0_ffn_centroids").T),
        wrouteT=np.ascontiguousarray(f("L0_ffn_w_route").T),
    )


def kernel(**inputs):
    x = np.ascontiguousarray(np.asarray(inputs["L0_attn_x"], dtype=np.float32))
    shared = make_shared(inputs)
    nc = build(debug=DEBUG)
    in_maps = []
    for b in range(8):
        m = dict(shared)
        m["x_fm"] = np.ascontiguousarray(x[b].T)
        in_maps.append(m)
    res = run_bass_kernel_spmd(nc, in_maps, core_ids=list(range(8)))
    out = np.stack([np.ascontiguousarray(np.asarray(r["out_fm"]).T)
                    for r in res.results])
    return out.astype(np.float32)
